# version 06
# baseline (speedup 1.0000x reference)
import numpy as np
import ml_dtypes
import concourse.bass as bass
import concourse.mybir as mybir
from concourse.bass_utils import run_bass_kernel_spmd
from concourse.ap import AP
from contextlib import ExitStack

AF = mybir.ActivationFunctionType
ALU = mybir.AluOpType
AXX = mybir.AxisListType.X
F32 = mybir.dt.float32
BF16 = mybir.dt.bfloat16
bfdt = ml_dtypes.bfloat16

S = 2048; DM = 1024; NDC = 8; NH = 16; DH = 64; NE = 8; FF = 4096
NTT = 4; TTW = 512; NTC = 16
UW = 4096; QW = 2560; KW = 2176; NLG = 4

DEV_TAPS = ()
DEV_STOP = None


class Sem:
    def __init__(self, nc, ctx, name):
        self.h = ctx.enter_context(nc.semaphore(name))
        self.n = 0


class Builder:
    def __init__(self):
        self.nc = bass.Bass("TRN2", target_bir_lowering=False)
        self.ctx = ExitStack()
        self.M = {}

    def sb(self, name, shape, dt=F32, stack=None, side=None):
        if side is None:
            side = "left" if stack is None else "right"
        return (stack or self.ctx).enter_context(
            self.nc.sbuf_tensor(name, shape, dt, side=side))

    def dram(self, name, shape, dt=F32):
        kind = "ExternalOutput" if name in DEV_TAPS else "Internal"
        return self.nc.dram_tensor(name, list(shape), dt, kind=kind)

    def I(self, sem, ins, k=1):
        sem.n += k
        ins.then_inc(sem.h, k)
        return sem.n

    def D(self, sem, ins):
        return self.I(sem, ins, 16)

    def OP(self, eng, sem, f):
        if sem.n:
            eng.wait_ge(sem.h, sem.n)
        ins = f()
        sem.n += 1
        ins.then_inc(sem.h, 1)
        return ins

    def mark(self, label, sem):
        self.M[label] = (sem, sem.n)

    def wt(self, eng, label):
        sem, n = self.M[label]
        if n > 0:
            eng.wait_ge(sem.h, n)

    def wait_all(self, eng):
        for s in self.all_sems:
            if s.n > 0:
                eng.wait_ge(s.h, s.n)

    def barrier(self):
        blk = self.block
        for deco in (blk.tensor, blk.scalar, blk.vector, blk.gpsimd, blk.sync):
            @deco
            def _(eng):
                self.wait_all(eng)

    def declare(self):
        nc = self.nc

        def din(name, shape):
            return nc.dram_tensor(name, list(shape), F32, kind="ExternalInput")
        i = self.inp = {}
        i["x"] = din("x", (S, DM))
        for n in ("Wq", "Wk", "Wv", "Wo", "mixw"):
            i[n] = din(n, (DM, DM))
        for n in ("bq", "bk", "bv", "bo", "lng", "lnb", "g1", "be1", "g2",
                  "be2", "mixb"):
            i[n] = din(n, (DM,))
        i["dexq"] = din("dexq", (DH, UW)); i["dexk"] = din("dexk", (DH, UW))
        i["rw"] = din("rw", (DM, NE))
        i["w1"] = din("w1", (NE, DM, FF)); i["b1"] = din("b1", (NE, FF))
        i["w2"] = din("w2", (NE, FF, DM)); i["b2"] = din("b2", (NE, DM))
        self.out_h = nc.dram_tensor("out", [S, DM], F32, kind="ExternalOutput")

        d = self.scr = {}
        d["xn"] = self.dram("xn", (NDC, 128, S))
        d["x2"] = self.dram("x2", (NDC, 128, S))
        d["sa"] = self.dram("sa", (NDC, 128, S))
        d["qs"] = self.dram("qs", (DM, S), BF16)
        d["cx"] = self.dram("cx", (NDC, 128, S), BF16)
        d["rq"] = self.dram("rq", (NH, S, 2304), BF16)
        d["rk"] = self.dram("rk", (NH, S, 2304), BF16)

        self.eye_h = nc.inline_tensor(np.eye(128, dtype=np.float32), "eyef")
        self.onesb_h = nc.inline_tensor(np.ones((128, 1), dtype=bfdt), "onespb")
        self.onesr_h = nc.inline_tensor(np.ones((1, 128), dtype=np.float32),
                                        "onesrf")
        sel_np = np.zeros((NE, NE * 128), dtype=np.float32)
        for e in range(NE):
            sel_np[e, e * 128:(e + 1) * 128] = 1.0
        self.sel8_h = nc.inline_tensor(sel_np, "sel8")

        self.eye = self.sb("eye", [128, 128], F32)
        self.onesb = self.sb("onesb", [128, 1], BF16)
        self.onesr = self.sb("onesr", [1, 128], F32)
        self.perp = self.sb("perp", [128, 8 * 11], F32)
        self.PERP = {n: k for k, n in enumerate(
            ["bq", "bk", "bv", "bo", "lng", "lnb", "g1", "be1", "g2", "be2",
             "mixb"])}
        self.b1perp = self.sb("b1perp", [128, NE * 32], F32)
        self.rw_sb = self.sb("rw_sb", [128, NDC * NE], F32)
        self.cst = self.sb("cst", [128, 2], F32)
        self.cstj = self.sb("cstj", [128, 2], F32)
        self.stBV = ExitStack()
        self.vstage = self.sb("vstage", [32, 128], F32, self.stBV)
        self.bvrow = self.sb("bvrow", [1, DM], F32, self.stBV)
        self.bvb = self.sb("bvb", [128, DM], F32, self.stBV)
        self.stKV = ExitStack()
        self.K_sb = self.sb("K_sb", [128, NDC * S], BF16, self.stKV,
                            side="left")
        self.VT = self.sb("VT", [128, NTC * NH * 65], BF16, self.stKV,
                          side="left")
        self.stW = ExitStack()
        wbuf1 = self.sb("wbuf1", [128, NDC * DM], BF16, self.stW,
                        side="left")
        self.stW0 = ExitStack()
        wbuf0 = self.sb("wbuf0", [128, NDC * DM], BF16, self.stW0,
                        side="left")
        self.wbuf = [wbuf0, wbuf1]
        self.stXNB = ExitStack()
        self.xnb = self.sb("xnb", [128, NDC * S], BF16, self.stXNB,
                           side="left")
        self.pb = [self.ctx.enter_context(
            self.nc.psum_tensor(f"pb{k}", [128, TTW], F32)) for k in range(8)]

        c = self.ctx
        self.LD = Sem(nc, c, "LD"); self.SX = Sem(nc, c, "SX")
        self.SC = Sem(nc, c, "SC"); self.PE = Sem(nc, c, "PE")
        self.VE = Sem(nc, c, "VE"); self.SE = Sem(nc, c, "SE")
        self.GP = Sem(nc, c, "GP"); self.ST = Sem(nc, c, "ST")
        self.SEX = Sem(nc, c, "SEX")
        self.SD = Sem(nc, c, "SD")
        self.all_sems = [self.LD, self.SX, self.SC, self.PE, self.VE,
                         self.SE, self.GP, self.ST, self.SEX, self.SD]
        self.block = self.ctx.enter_context(nc.Block())

    def pcol(self, name, oc):
        k = self.PERP[name]
        return self.perp[:, k * 8 + oc: k * 8 + oc + 1]

    def consts(self):
        blk = self.block
        i = self.inp
        LD, SX, PE, VE, SE = self.LD, self.SX, self.PE, self.VE, self.SE
        vstage = self.vstage
        bvrow = self.bvrow
        vecs = ["bq", "bk", "bv", "bo", "lng", "lnb", "g1", "be1", "g2",
                "be2", "mixb"]
        NV = len(vecs)
        LD0, SX0, PE0, VE0, SE0 = LD.n, SX.n, PE.n, VE.n, SE.n

        @blk.gpsimd
        def _(g):
            for dst, src in [(self.eye, self.eye_h), (self.onesb, self.onesb_h),
                             (self.onesr, self.onesr_h)]:
                self.D(LD, g.dma_start(out=dst[:], in_=src.ap()))
            for dc in range(NDC):
                self.D(LD, g.dma_start(
                    out=self.rw_sb[:, dc * NE:(dc + 1) * NE],
                    in_=AP(i["rw"], dc * 128 * NE, [[NE, 128], [1, NE]]),
                    allow_slow_non_contiguous=True))

        @blk.sync
        def _(sy):
            for k in range(NV):
                if k >= 1:
                    sy.wait_ge(PE.h, PE0 + k)
                self.D(SX, sy.dma_start(
                    out=vstage[0:8, :],
                    in_=AP(i[vecs[k]], 0, [[128, 8], [1, 128]])))
            for e in range(NE):
                sy.wait_ge(PE.h, PE0 + NV + e)
                self.D(SX, sy.dma_start(
                    out=vstage[:],
                    in_=AP(i["b1"], e * FF, [[128, 32], [1, 128]])))
            self.D(SX, sy.dma_start(
                out=bvrow[:], in_=AP(i["bv"], 0, [[DM, 1], [1, DM]])))

        @blk.tensor
        def _(t):
            t.wait_ge(LD.h, LD0 + 16 * 3)
            for k in range(NV):
                t.wait_ge(SX.h, SX0 + 16 * (k + 1))
                if k >= 2:
                    t.wait_ge(VE.h, VE0 + 2 + k - 1)
                self.I(PE, t.transpose(self.pb[k % 2][:, 0:8], vstage[0:8, :],
                                       self.eye[0:8, 0:8]))
            for e in range(NE):
                t.wait_ge(SX.h, SX0 + 16 * (NV + e + 1))
                if e >= 2:
                    t.wait_ge(VE.h, VE0 + 2 + NV + e - 1)
                self.I(PE, t.transpose(self.pb[(NV + e) % 2][:, 0:32],
                                       vstage[:], self.eye[0:32, 0:32]))
            t.wait_ge(SX.h, SX0 + 16 * (NV + NE + 1))
            for half in range(2):
                self.I(PE, t.matmul(self.pb[2 + half][:], self.onesr[:],
                                    bvrow[:, half * 512:(half + 1) * 512],
                                    start=True, stop=True))

        @blk.vector
        def _(v):
            self.OP(v, VE, lambda: v.memset(self.cst[:, 0:1], 1e-5))
            self.OP(v, VE, lambda: v.memset(self.cst[:, 1:2], 1e-12))
            for k in range(NV):
                v.wait_ge(PE.h, PE0 + k + 1)
                kk = self.PERP[vecs[k]]
                self.OP(v, VE, lambda: v.tensor_copy(
                    self.perp[:, kk * 8:(kk + 1) * 8], self.pb[k % 2][:, 0:8]))
            for e in range(NE):
                v.wait_ge(PE.h, PE0 + NV + e + 1)
                self.OP(v, VE, lambda: v.tensor_copy(
                    self.b1perp[:, e * 32:(e + 1) * 32],
                    self.pb[(NV + e) % 2][:, 0:32]))

        @blk.scalar
        def _(s):
            for half in range(2):
                s.wait_ge(PE.h, PE0 + NV + NE + half + 1)
                self.OP(s, SE, lambda: s.activation(
                    self.bvb[:, half * 512:(half + 1) * 512],
                    self.pb[2 + half][:], AF.Copy))

        self.barrier()

    def p0_preln(self):
        blk = self.block
        SX, SC, PE, VE, SE = self.SX, self.SC, self.PE, self.VE, self.SE
        st = ExitStack()
        xin = [self.sb(f"xin{k}", [128, DM], F32, st) for k in range(2)]
        xsq = self.sb("xsq", [128, DM], F32, st)
        xst = self.sb("xst", [128, 8], F32, st)
        xnt = self.sb("xnt", [128, DM], F32, st)
        stg = [self.sb(f"xnstg{k}", [128, DM], F32, st) for k in range(2)]
        SX0, SC0, PE0, VE0, SE0 = SX.n, SC.n, PE.n, VE.n, SE.n

        @blk.vector
        def _(v):
            def chain(tc):
                v.wait_ge(SX.h, SX0 + 16 * (tc + 1))
                b = xin[tc % 2]
                for f in [
                    lambda: v.reduce_sum(xst[:, 0:1], b[:], axis=AXX),
                    lambda: v.tensor_tensor(xsq[:], b[:], b[:], ALU.mult),
                    lambda: v.reduce_sum(xst[:, 1:2], xsq[:], axis=AXX),
                    lambda: v.tensor_scalar(xst[:, 2:3], xst[:, 0:1],
                                            1.0 / DM, None, ALU.mult),
                    lambda: v.tensor_scalar(xst[:, 3:4], xst[:, 1:2],
                                            1.0 / DM, None, ALU.mult),
                    lambda: v.tensor_tensor(xst[:, 4:5], xst[:, 2:3],
                                            xst[:, 2:3], ALU.mult),
                    lambda: v.scalar_tensor_tensor(xst[:, 5:6], xst[:, 4:5],
                                                   -1.0, xst[:, 3:4],
                                                   ALU.mult, ALU.add),
                ]:
                    self.OP(v, VE, f)
                self.mark(("p0var", tc), VE)
                v.wait_ge(SE.h, SE0 + 2 * (tc + 1))
                self.OP(v, VE, lambda: v.reciprocal(xst[:, 7:8], xst[:, 6:7]))
                if tc >= 1:
                    v.wait_ge(PE.h, PE0 + 2 * tc)
                self.OP(v, VE, lambda: v.tensor_scalar(
                    xnt[:], b[:], xst[:, 2:3], xst[:, 7:8],
                    ALU.subtract, ALU.mult))
                self.mark(("p0chain", tc), VE)

            def evac(tc):
                bankbase = 4 + 2 * (tc % 2)
                v.wait_ge(PE.h, PE0 + 2 * tc + 2)
                if tc >= 2:
                    v.wait_ge(SC.h, SC0 + 16 * (tc - 1))
                for half in range(2):
                    self.OP(v, VE, lambda: v.tensor_copy(
                        stg[tc % 2][:, half * 512:(half + 1) * 512],
                        self.pb[bankbase + half][:]))
                self.mark(("p0stg", tc), VE)
                for dc in range(NDC):
                    bank = bankbase + dc // 4
                    self.OP(v, VE, lambda: v.tensor_copy(
                        self.xnb[:, dc * S + tc * 128: dc * S + (tc + 1) * 128],
                        self.pb[bank][:, (dc % 4) * 128:(dc % 4 + 1) * 128]))
                self.mark(("p0evac", tc), VE)

            for tc in range(NTC):
                chain(tc)
                if tc >= 1:
                    evac(tc - 1)
            evac(NTC - 1)

        @blk.scalar
        def _(s):
            for tc in range(NTC):
                self.wt(s, ("p0var", tc))
                self.OP(s, SE, lambda: s.activation(xst[:, 6:7], xst[:, 5:6],
                                                    AF.Sqrt,
                                                    bias=self.cst[:, 0:1]))
                self.OP(s, SE, lambda: s.activation(xst[:, 4:5],
                                                    self.cst[:, 0:1], AF.Copy))

        @blk.tensor
        def _(t):
            for tc in range(NTC):
                self.wt(t, ("p0chain", tc))
                if tc >= 2:
                    self.wt(t, ("p0evac", tc - 2))
                bankbase = 4 + 2 * (tc % 2)
                for dc in range(NDC):
                    bank = bankbase + dc // 4
                    mm = t.transpose(
                        self.pb[bank][:, (dc % 4) * 128:(dc % 4 + 1) * 128],
                        xnt[:, dc * 128:(dc + 1) * 128], self.eye[:])
                    if dc % 4 == 3:
                        self.I(PE, mm)

        @blk.sync
        def _(sy):
            for tc in range(NTC + 1):
                if tc < NTC:
                    if tc >= 2:
                        self.wt(sy, ("p0chain", tc - 2))
                    self.D(SX, sy.dma_start(
                        out=xin[tc % 2][:],
                        in_=self.inp["x"].ap()[tc * 128:(tc + 1) * 128, :]))
                if tc >= 1:
                    self.wt(sy, ("p0stg", tc - 1))
                    dst = self.scr["xn"].ap().rearrange("a p t -> p a t")[
                        :, :, (tc - 1) * 128: tc * 128]
                    self.D(SC, sy.dma_start(
                        out=dst, in_=stg[(tc - 1) % 2][:].rearrange(
                            "p (a t) -> p a t", t=128)))

        self.barrier()
        st.close()

    def p1_qkv(self):
        blk = self.block
        i = self.inp
        LD, SC, PE, VE, SE = self.LD, self.SC, self.PE, self.VE, self.SE
        st = ExitStack()
        qstg = [self.sb(f"qstg{k}", [128, TTW], BF16, st) for k in range(2)]
        LD0, SC0, PE0, VE0, SE0 = LD.n, SC.n, PE.n, VE.n, SE.n

        @blk.gpsimd
        def _(g):
            for dc in range(NDC):
                self.D(LD, g.dma_start(
                    out=self.wbuf[0][:, dc * DM:(dc + 1) * DM],
                    in_=i["Wq"].ap()[dc * 128:(dc + 1) * 128, :]))
            for dc in range(NDC):
                self.D(LD, g.dma_start(
                    out=self.wbuf[1][:, dc * DM:(dc + 1) * DM],
                    in_=i["Wk"].ap()[dc * 128:(dc + 1) * 128, :]))
            g.wait_ge(PE.h, PE0 + 8)
            for dc in range(NDC):
                self.D(LD, g.dma_start(
                    out=self.wbuf[0][:, dc * DM:(dc + 1) * DM],
                    in_=i["Wv"].ap()[dc * 128:(dc + 1) * 128, :]))

        @blk.vector
        def _(v):
            ap = self.VT[:].rearrange("p (a c) -> p a c", c=65)[:, :, 64:65]
            self.OP(v, VE, lambda: v.memset(ap, 1.0))

        def proj_pe(t, wb, ldcount, se_gate):
            for oc in range(NDC):
                t.wait_ge(LD.h, LD0 + 16 * ldcount)
                if se_gate(oc) is not None:
                    t.wait_ge(SE.h, se_gate(oc))
                for dc in range(NDC):
                    lhsT = wb[:, dc * DM + oc * 128: dc * DM + (oc + 1) * 128]
                    for tt in range(NTT):
                        mm = t.matmul(
                            self.pb[(oc % 2) * 4 + tt][:], lhsT,
                            self.xnb[:, dc * S + tt * TTW:
                                     dc * S + (tt + 1) * TTW],
                            start=(dc == 0), stop=(dc == NDC - 1))
                self.I(PE, mm)

        @blk.tensor
        def _(t):
            proj_pe(t, self.wbuf[0], 8,
                    lambda oc: SE0 + 4 * (oc - 1) if oc >= 2 else None)

        @blk.scalar
        def _(s):
            for oc in range(NDC):
                s.wait_ge(PE.h, PE0 + oc + 1)
                for tt in range(NTT):
                    k = oc * NTT + tt
                    if k >= 2:
                        s.wait_ge(SC.h, SC0 + 16 * (k - 1))
                    self.OP(s, SE, lambda: s.activation(
                        qstg[k % 2][:], self.pb[(oc % 2) * 4 + tt][:],
                        AF.Identity, bias=self.pcol("bq", oc)))
                    s.wait_ge(SE.h, SE.n)
                    self.D(SC, s.dma_start(
                        out=self.scr["qs"].ap()[oc * 128:(oc + 1) * 128,
                                                tt * TTW:(tt + 1) * TTW],
                        in_=qstg[k % 2][:]))

        @blk.tensor
        def _(t):
            proj_pe(t, self.wbuf[1], 16,
                    lambda oc: (SE0 + 4 * (oc - 1) + 32 if oc >= 2
                                else SE0 + 32))

        @blk.scalar
        def _(s):
            for oc in range(NDC):
                s.wait_ge(PE.h, PE0 + 8 + oc + 1)
                for tt in range(NTT):
                    self.OP(s, SE, lambda: s.activation(
                        self.K_sb[:, oc * S + tt * TTW: oc * S + (tt + 1) * TTW],
                        self.pb[(oc % 2) * 4 + tt][:], AF.Identity,
                        bias=self.pcol("bk", oc)))

        PE_V0 = PE0 + 16
        @blk.tensor
        def _(t):
            t.wait_ge(LD.h, LD0 + 16 * 24)
            t.wait_ge(SE.h, SE0 + 64)
            for tc in range(NTC):
                if tc >= 2:
                    t.wait_ge(VE.h, VE0 + 1 + 2 * (tc - 1))
                for half in range(2):
                    for dc in range(NDC):
                        lhsT = self.xnb[:, dc * S + tc * 128:
                                        dc * S + (tc + 1) * 128]
                        mm = t.matmul(
                            self.pb[(tc % 2) * 2 + half][:], lhsT,
                            self.wbuf[0][:, dc * DM + half * 512:
                                         dc * DM + (half + 1) * 512],
                            start=(dc == 0), stop=(dc == NDC - 1))
                    self.I(PE, mm)

        @blk.vector
        def _(v):
            for tc in range(NTC):
                for half in range(2):
                    v.wait_ge(PE.h, PE_V0 + 2 * tc + half + 1)
                    src = self.pb[(tc % 2) * 2 + half][:].rearrange(
                        "p (a c) -> p a c", c=64)
                    d0 = (tc * NH + half * 8) * 65
                    dst = self.VT[:, d0: d0 + 8 * 65].rearrange(
                        "p (a c) -> p a c", c=65)[:, :, 0:64]
                    self.OP(v, VE, lambda: v.scalar_tensor_tensor(
                        dst, src, 1.0,
                        self.bvb[:, half * 512:(half + 1) * 512].rearrange(
                            "p (a c) -> p a c", c=64),
                        ALU.mult, ALU.add))

        self.barrier()
        st.close()

    def p2_front(self):
        blk = self.block
        i = self.inp
        LD, SX, SC, PE, VE, SE = (self.LD, self.SX, self.SC, self.PE,
                                  self.VE, self.SE)
        self.stP2 = ExitStack()
        self.n2b = self.sb("n2b", [128, NDC * S], BF16, self.stP2)
        self.comb = self.sb("comb", [NE, S], F32, self.stP2)
        st = ExitStack()
        xns = self.sb("xns", [128, NDC * TTW], F32, st)
        nb = self.sb("nb", [128, NDC * TTW], BF16, st)
        ntmp = self.sb("ntmp", [128, TTW], F32, st)
        n2t = self.sb("n2t", [128, TTW], F32, st)
        sqb = [self.sb(f"sqb{k}", [128, TTW], BF16, st) for k in range(2)]
        rws = self.sb("rws", [1, 3 * TTW], F32, st)
        rlog = ntmp[0:NE, :]
        lgT = n2t[:, 0:48]
        combsub = n2t[:, 48:80]
        smt = n2t[:, 80:120]
        LD0 = LD.n

        @blk.gpsimd
        def _(g):
            for dc in range(NDC):
                self.D(LD, g.dma_start(
                    out=self.wbuf[0][:, dc * DM:(dc + 1) * DM],
                    in_=i["mixw"].ap()[dc * 128:(dc + 1) * 128, :]))
        self.mark("p2mixw", LD)

        for tt in range(NTT):
            self._p2_tt(tt, xns, nb, ntmp, n2t, sqb, rws, rlog, lgT, smt,
                        combsub)

        self.barrier()
        st.close()

    def _p2_rows(self, tag, tt, rws, src0, src1):
        blk = self.block
        PE, VE, SE = self.PE, self.VE, self.SE
        MU = rws[:, 0:TTW]; TMP = rws[:, TTW:2 * TTW]
        BB = rws[:, 2 * TTW:3 * TTW]; JNK = self.cstj[0:1, 0:1]

        @blk.vector
        def _(v):
            self.wt(v, (tag + "st", tt))
            self.OP(v, VE, lambda: v.tensor_scalar(MU, src0, 1.0 / DM, None,
                                                   ALU.mult))
            self.OP(v, VE, lambda: v.tensor_scalar(TMP, src1, 1.0 / DM, None,
                                                   ALU.mult))
            self.OP(v, VE, lambda: v.tensor_tensor(BB, MU, MU, ALU.mult))
            self.OP(v, VE, lambda: v.scalar_tensor_tensor(
                TMP, BB, -1.0, TMP, ALU.mult, ALU.add))
            self.mark((tag + "var", tt), VE)

        @blk.scalar
        def _(s):
            self.wt(s, (tag + "var", tt))
            self.OP(s, SE, lambda: s.activation(TMP, TMP, AF.Sqrt,
                                                bias=self.cst[0:1, 0:1]))
            self.OP(s, SE, lambda: s.activation(JNK, self.cst[0:1, 0:1],
                                                AF.Copy))
            self.mark((tag + "sd", tt), SE)

        @blk.vector
        def _(v):
            self.wt(v, (tag + "sd", tt))
            self.OP(v, VE, lambda: v.reciprocal(TMP, TMP))
            self.OP(v, VE, lambda: v.scalar_tensor_tensor(
                BB, MU, -1.0, TMP, ALU.mult, ALU.mult))
            self.mark((tag + "rows", tt), VE)

        @blk.tensor
        def _(t):
            self.wt(t, (tag + "rows", tt))
            self.I(PE, t.matmul(self.pb[2][:], self.onesr[:], TMP,
                                start=True, stop=True))
            self.I(PE, t.matmul(self.pb[3][:], self.onesr[:], BB,
                                start=True, stop=True))
            self.mark((tag + "bc", tt), PE)

    def _p2_tt(self, tt, xns, nb, ntmp, n2t, sqb, rws, rlog, lgT, smt,
               combsub):
        blk = self.block
        LD, SX, SC, PE, VE, SE = (self.LD, self.SX, self.SC, self.PE,
                                  self.VE, self.SE)
        pb = self.pb
        MU = rws[:, 0:TTW]; TMP = rws[:, TTW:2 * TTW]

        @blk.sync
        def _(sy):
            if SC.n:
                sy.wait_ge(SC.h, SC.n)
            if tt >= 1:
                self.wt(sy, ("p2n2", tt - 1, NDC - 1))
            for dc in range(NDC):
                self.D(SX, sy.dma_start(
                    out=xns[:, dc * TTW:(dc + 1) * TTW],
                    in_=self.scr["xn"].ap()[dc, :, tt * TTW:(tt + 1) * TTW]))
            self.mark(("p2xn", tt), SX)

        @blk.tensor
        def _(t):
            if tt == 0:
                self.wt(t, "p2mixw")
            if ("p2n2done", tt - 1) in self.M:
                self.wt(t, ("p2n2done", tt - 1))
            for dc in range(NDC):
                mm = t.matmul(pb[0][0:1, :], self.onesb[:],
                              self.xnb[:, dc * S + tt * TTW:
                                       dc * S + (tt + 1) * TTW],
                              start=(dc == 0), stop=(dc == NDC - 1))
            self.I(PE, mm)
        for dc in range(NDC):
            @blk.scalar
            def _(s):
                if dc >= 2:
                    self.wt(s, ("p2sq1mm", tt, dc - 2))
                self.OP(s, SE, lambda: s.activation(
                    sqb[dc % 2][:],
                    self.xnb[:, dc * S + tt * TTW: dc * S + (tt + 1) * TTW],
                    AF.Square))
                self.mark(("p2sq1", tt, dc), SE)

            @blk.tensor
            def _(t):
                self.wt(t, ("p2sq1", tt, dc))
                mm = t.matmul(pb[1][0:1, :], self.onesb[:], sqb[dc % 2][:],
                              start=(dc == 0), stop=(dc == NDC - 1))
                self.I(PE, mm)
                self.mark(("p2sq1mm", tt, dc), PE)
        self.mark(("p2ln1st", tt), PE)

        self._p2_rows("p2ln1", tt, rws, pb[0][0:1, :], pb[1][0:1, :])

        @blk.vector
        def _(v):
            self.wt(v, ("p2ln1bc", tt))
            self.wt(v, ("p2xn", tt))
            for dc in range(NDC):
                xd = xns[:, dc * TTW:(dc + 1) * TTW]
                self.OP(v, VE, lambda: v.tensor_tensor(ntmp[:], xd, pb[2][:],
                                                       ALU.mult))
                self.OP(v, VE, lambda: v.tensor_tensor(ntmp[:], ntmp[:],
                                                       pb[3][:], ALU.add))
                self.OP(v, VE, lambda: v.tensor_scalar(
                    ntmp[:], ntmp[:], self.pcol("g1", dc),
                    self.pcol("be1", dc), ALU.mult, ALU.add))
                self.OP(v, VE, lambda: v.tensor_copy(
                    nb[:, dc * TTW:(dc + 1) * TTW], ntmp[:]))
                self.OP(v, VE, lambda: v.tensor_tensor(xd, xd, ntmp[:],
                                                       ALU.add))
            self.mark(("p2nb", tt), VE)

        for oc in range(NDC):
            @blk.tensor
            def _(t):
                if oc == 0:
                    self.wt(t, ("p2nb", tt))
                if oc >= 2:
                    self.wt(t, ("p2x2e", tt, oc - 2))
                for dc in range(NDC):
                    mm = t.matmul(pb[4 + oc % 2][:],
                                  self.wbuf[0][:, dc * DM + oc * 128:
                                               dc * DM + (oc + 1) * 128],
                                  nb[:, dc * TTW:(dc + 1) * TTW],
                                  start=(dc == 0), stop=(dc == NDC - 1))
                self.I(PE, mm)
                self.mark(("p2mix", tt, oc), PE)

            @blk.scalar
            def _(s):
                self.wt(s, ("p2mix", tt, oc))
                if oc >= 1:
                    self.wt(s, ("p2x2", tt, oc - 1))
                self.OP(s, SE, lambda: s.activation(
                    ntmp[:], pb[4 + oc % 2][:], AF.Identity,
                    bias=self.pcol("mixb", oc)))
                self.mark(("p2x2e", tt, oc), SE)

            @blk.vector
            def _(v):
                self.wt(v, ("p2x2e", tt, oc))
                if oc >= 1:
                    self.wt(v, ("p2sq2", tt, oc - 1))
                    self.wt(v, ("p2st2a", tt, oc - 1))
                xd = xns[:, oc * TTW:(oc + 1) * TTW]
                self.OP(v, VE, lambda: v.tensor_tensor(xd, xd, ntmp[:],
                                                       ALU.add))
                self.mark(("p2x2", tt, oc), VE)
                self.OP(v, VE, lambda: v.tensor_copy(sqb[0][:], xd))
                self.mark(("p2x2b", tt, oc), VE)

            @blk.tensor
            def _(t):
                self.wt(t, ("p2x2b", tt, oc))
                mm = t.matmul(pb[0][0:1, :], self.onesb[:], sqb[0][:],
                              start=(oc == 0), stop=(oc == NDC - 1))
                self.I(PE, mm)
                self.mark(("p2st2a", tt, oc), PE)

            @blk.scalar
            def _(s):
                self.wt(s, ("p2x2b", tt, oc))
                if oc >= 2:
                    self.wt(s, ("p2sq2mm", tt, oc - 1))
                self.OP(s, SE, lambda: s.activation(sqb[1][:], sqb[0][:],
                                                    AF.Square))
                self.mark(("p2sq2", tt, oc), SE)

            @blk.tensor
            def _(t):
                self.wt(t, ("p2sq2", tt, oc))
                self.wt(t, ("p2st2a", tt, oc))
                mm = t.matmul(pb[1][0:1, :], self.onesb[:], sqb[1][:],
                              start=(oc == 0), stop=(oc == NDC - 1))
                self.I(PE, mm)
                self.mark(("p2sq2mm", tt, oc), PE)

            @blk.sync
            def _(sy):
                self.wt(sy, ("p2x2", tt, oc))
                self.D(SC, sy.dma_start(
                    out=self.scr["x2"].ap()[oc, :, tt * TTW:(tt + 1) * TTW],
                    in_=xns[:, oc * TTW:(oc + 1) * TTW]))
        self.mark(("p2ln2st", tt), PE)

        self._p2_rows("p2ln2", tt, rws, pb[0][0:1, :], pb[1][0:1, :])

        for dc in range(NDC):
            @blk.vector
            def _(v):
                if dc == 0:
                    self.wt(v, ("p2ln2bc", tt))
                else:
                    self.wt(v, ("p2rmm", tt, dc - 1))
                xd = xns[:, dc * TTW:(dc + 1) * TTW]
                self.OP(v, VE, lambda: v.tensor_tensor(n2t[:], xd, pb[2][:],
                                                       ALU.mult))
                self.OP(v, VE, lambda: v.tensor_tensor(n2t[:], n2t[:],
                                                       pb[3][:], ALU.add))
                self.OP(v, VE, lambda: v.tensor_scalar(
                    n2t[:], n2t[:], self.pcol("g2", dc),
                    self.pcol("be2", dc), ALU.mult, ALU.add))
                self.OP(v, VE, lambda: v.tensor_copy(
                    self.n2b[:, dc * S + tt * TTW: dc * S + (tt + 1) * TTW],
                    n2t[:]))
                self.mark(("p2n2", tt, dc), VE)

            @blk.tensor
            def _(t):
                self.wt(t, ("p2n2", tt, dc))
                mm = t.matmul(pb[6][0:NE, :],
                              self.rw_sb[:, dc * NE:(dc + 1) * NE],
                              n2t[:], start=(dc == 0), stop=(dc == NDC - 1))
                self.I(PE, mm)
                self.mark(("p2rmm", tt, dc), PE)
        self.mark(("p2n2done", tt), VE)

        @blk.vector
        def _(v):
            self.wt(v, ("p2rmm", tt, NDC - 1))
            self.OP(v, VE, lambda: v.tensor_copy(rlog[:], pb[6][0:NE, :]))
            self.mark(("p2rlog", tt), VE)

        @blk.tensor
        def _(t):
            self.wt(t, ("p2rlog", tt))
            for c in range(4):
                self.I(PE, t.transpose(pb[7][:, c * 8:(c + 1) * 8],
                                       rlog[:, c * 128:(c + 1) * 128],
                                       self.eye[0:NE, 0:NE]))
            self.mark(("p2lgT", tt), PE)

        @blk.vector
        def _(v):
            self.wt(v, ("p2lgT", tt))
            self.OP(v, VE, lambda: v.tensor_copy(lgT[:, 16:48],
                                                 pb[7][:, 0:32]))
            for c in range(4):
                L = lgT[:, 16 + c * 8:24 + c * 8]
                self.OP(v, VE, lambda: v.reduce_max(lgT[:, c:c + 1], L,
                                                    axis=AXX))
                self.OP(v, VE, lambda: v.tensor_scalar(
                    lgT[:, 4 + c:5 + c], lgT[:, c:c + 1], -1.0, None,
                    ALU.mult))
            self.mark(("p2negm", tt), VE)

        @blk.scalar
        def _(s):
            self.wt(s, ("p2negm", tt))
            for c in range(4):
                self.OP(s, SE, lambda: s.activation(
                    smt[:, c * 8:(c + 1) * 8],
                    lgT[:, 16 + c * 8:24 + c * 8], AF.Exp,
                    bias=lgT[:, 4 + c:5 + c]))
            self.mark(("p2exp", tt), SE)

        @blk.vector
        def _(v):
            self.wt(v, ("p2exp", tt))
            for c in range(4):
                e = smt[:, c * 8:(c + 1) * 8]
                ssum = lgT[:, 8:9]; rs = lgT[:, 9:10]
                m1 = lgT[:, 10:11]; m2 = lgT[:, 11:12]
                probs = lgT[:, 16 + c * 8:24 + c * 8]
                msk = smt[:, 32:40]
                self.OP(v, VE, lambda: v.reduce_sum(ssum, e, axis=AXX))
                self.OP(v, VE, lambda: v.reciprocal(rs, ssum))
                self.OP(v, VE, lambda: v.tensor_scalar(probs, e, rs, None,
                                                       ALU.mult))
                self.OP(v, VE, lambda: v.reduce_max(m1, probs, axis=AXX))
                self.OP(v, VE, lambda: v.tensor_scalar(msk, probs, m1, None,
                                                       ALU.is_ge))
                self.OP(v, VE, lambda: v.scalar_tensor_tensor(
                    msk, msk, -2.0, probs, ALU.mult, ALU.add))
                self.OP(v, VE, lambda: v.reduce_max(m2, msk, axis=AXX))
                self.OP(v, VE, lambda: v.tensor_scalar(msk, probs, m2, None,
                                                       ALU.is_ge))
                self.OP(v, VE, lambda: v.tensor_tensor(
                    combsub[:, c * 8:(c + 1) * 8], probs, msk, ALU.mult))
            self.mark(("p2comb", tt), VE)

        @blk.tensor
        def _(t):
            self.wt(t, ("p2comb", tt))
            for c in range(4):
                self.I(PE, t.transpose(pb[5][0:8, c * 128:(c + 1) * 128],
                                       combsub[:, c * 8:(c + 1) * 8],
                                       self.eye[:]))
            self.mark(("p2combT", tt), PE)

        @blk.vector
        def _(v):
            self.wt(v, ("p2combT", tt))
            self.OP(v, VE, lambda: v.tensor_copy(
                self.comb[:, tt * TTW:(tt + 1) * TTW], pb[5][0:8, :]))
            self.mark(("p2combout", tt), VE)

    def p3_attn(self):
        blk = self.block
        i = self.inp
        LD, SX, SC, PE, VE, SE = (self.LD, self.SX, self.SC, self.PE,
                                  self.VE, self.SE)
        st = self.stP3 = ExitStack()
        self.dexq = self.sb("dexq_s", [128, UW], BF16, st)
        self.dexk = self.sb("dexk_s", [128, UW], BF16, st)
        qh = [self.sb(f"qh{k}", [128, S], BF16, st) for k in range(2)]
        qxs = [self.sb(f"qxs{k}", [128, KW], BF16, st) for k in range(2)]
        kxs = [self.sb(f"kxs{k}", [128, KW], BF16, st) for k in range(2)]
        rstrip = ([self.sb(f"rqs{k}", [128, S], BF16, st) for k in range(3)],
                  [self.sb(f"rks{k}", [128, S], BF16, st) for k in range(3)])
        s2f = self.sb("s2fb", [128, 8 * TTW], BF16, st)
        prb = self.sb("prbb", [128, 8 * TTW], BF16, st)
        denr = self.sb("denr", [1, TTW], F32, st)
        denb = self.sb("denb", [DH, TTW], BF16, st)
        ctxo = self.sb("ctxo", [DH, TTW], BF16, st)
        c8 = self.sb("c8", [1, 2], F32, st)
        cb = self.sb("cb", [128, 1], F32, st)

        @blk.gpsimd
        def _(g):
            for half in range(2):
                b0 = half * 64
                self.D(LD, g.dma_start(out=self.dexq[b0:b0 + 64, :],
                                       in_=i["dexq"].ap()))
                self.D(LD, g.dma_start(out=self.dexk[b0:b0 + 64, :],
                                       in_=i["dexk"].ap()))
        self.mark("p3dex", LD)

        @blk.vector
        def _(v):
            v.engine_nop().then_inc(self.VE.h, 1)
            self.VE.n += 1
            self.mark("p3ones", VE)

        self.megaq = []
        self.megax = []
        self.p3qxs = qxs
        for h in range(NH):
            self._p3_gen(h, qh, qxs, kxs)
            self._p3_scores(h, qh, rstrip, s2f, prb, denr, denb, ctxo,
                            c8, cb)
        self.barrier()

    def _p3_gen(self, h, qh, qxs, kxs):
        blk = self.block
        SX, SC, PE, VE, SE = self.SX, self.SC, self.PE, self.VE, self.SE
        pb = self.pb
        b0 = (h % 2) * 64
        kslice = lambda c0, c1: self.K_sb[
            b0:b0 + 64, (h // 2) * S + c0:(h // 2) * S + c1]
        WIN = 2175
        pw = [512, 512, 512, 512, 127]

        @blk.sync
        def _(sy):
            if h >= 2:
                self.wt(sy, ("p3qdone", h - 2))
            self.D(SX, sy.dma_start(out=qh[h % 2][b0:b0 + 64, :],
                                    in_=self.scr["qs"].ap()[
                                        h * DH:(h + 1) * DH, :]))
            self.mark(("p3qh", h), SX)

        pgc = [0]
        for which, dstd, srcq in (("q", self.scr["rq"], True),
                                  ("k", self.scr["rk"], False)):
            stg = qxs if srcq else kxs
            for C in range(NTC):
                u0 = 1920 - 128 * C
                w = h * NTC + C
                pgs = []
                @blk.tensor
                def _(t):
                    for f in range(5):
                        pg = pgc[0]; pgc[0] += 1
                        pgs.append(pg)
                        if h == 0 and pg == 0:
                            self.wt(t, "p3dex")
                        if pg == 0:
                            if srcq:
                                self.wt(t, ("p3qh", h))
                            if h >= 1:
                                self.wt(t, ("p3ctxo", h - 1, NLG - 1))
                                self.wt(t, ("p3dene", h - 1, NLG - 1))
                        if pg >= 6:
                            if pg - 6 < 80:
                                t.wait_ge(self.SEX.h, h * 96 + pg - 5)
                            else:
                                self.wt(t, ("p3ge", h, pg - 6))
                        lhsT = (qh[h % 2][b0:b0 + 64,
                                          C * 128:(C + 1) * 128]
                                if srcq else kslice(C * 128, (C + 1) * 128))
                        tbl = self.dexq if srcq else self.dexk
                        j0 = f * 512
                        mm = t.matmul(
                            pb[2 + pg % 6][:, 0:pw[f]], lhsT,
                            tbl[b0:b0 + 64, u0 + j0: u0 + j0 + pw[f]],
                            start=True, stop=True)
                        self.I(PE, mm)
                        self.mark(("p3gm", h, pg), PE)

                if srcq:
                    for k, pg in enumerate(pgs):
                        self.megaq.append((w, k, pg))
                elif False:
                    pass
                else:
                    @blk.vector
                    def _(v):
                        if w >= 2:
                            self.wt(v, ("p3gw", which, w - 2))
                        for k, pg in enumerate(pgs):
                            self.wt(v, ("p3gm", h, pg))
                            self.OP(v, VE, lambda: v.tensor_copy(
                                stg[w % 2][:, k * 512: k * 512 + pw[k]],
                                pb[2 + pg % 6][:, 0:pw[k]]))
                            self.mark(("p3ge", h, pg), VE)

                @blk.sync
                def _(sy):
                    if srcq:
                        sy.wait_ge(self.SEX.h, h * 96 + pgs[-1] + 1)
                    else:
                        self.wt(sy, ("p3ge", h, pgs[-1]))
                    dst = AP(dstd, h * S * 2304 + C * 128 * 2304 + 1,
                             [[2305, 128], [1, WIN]])
                    self.D(SC, sy.dma_start(out=dst,
                                            in_=stg[w % 2][:, 0:WIN]))
                    self.mark(("p3gw", which, w), SC)
            self.mark(("p3genq" if srcq else "p3genk", h), SC)

    def _p3_scores(self, h, qh, rstrip, s2f, prb, denr, denb, ctxo, c8, cb):
        blk = self.block
        SX, SC, PE, VE, SE = self.SX, self.SC, self.PE, self.VE, self.SE
        pb = self.pb
        b0 = (h % 2) * 64
        kslice = lambda c0, c1: self.K_sb[
            b0:b0 + 64, (h // 2) * S + c0:(h // 2) * S + c1]
        NR = 2

        for rc in range(NTC):
            R0 = rc * 128
            @blk.sync
            def _(sy):
                if rc >= NR:
                    self.wt(sy, ("p3s2", (h * NTC + rc - NR) * NLG + 3))
                self.wt(sy, ("p3genq", h))
                for piece in range(4):
                    self.D(SX, sy.dma_start(
                        out=rstrip[0][rc % NR][:, piece * 512:
                                               (piece + 1) * 512],
                        in_=AP(self.scr["rq"],
                               h * S * 2304 + (piece * 512) * 2304
                               + 128 + R0,
                               [[2304, 512], [1, 128]]),
                        transpose=True))
                self.mark(("p3rdq", h * NTC + rc), SX)

            @blk.sync
            def _(sq):
                if rc >= NR:
                    self.wt(sq, ("p3s2", (h * NTC + rc - NR) * NLG + 3))
                self.wt(sq, ("p3genk", h))
                self.D(self.SD, sq.dma_start(
                    out=rstrip[1][rc % NR][:],
                    in_=AP(self.scr["rk"], h * S * 2304 + R0 * 2304 + 128,
                           [[2304, 128], [1, S]])))
                self.mark(("p3rd", h * NTC + rc), self.SD)

            rq, rk = rstrip
            for lg in range(NLG):
                L0 = lg * TTW
                g = (h * NTC + rc) * NLG + lg

                @blk.tensor
                def _(t):
                    if rc == 0:
                        self.wt(t, ("p3genq", h))
                        self.wt(t, ("p3genk", h))
                    if g >= 2:
                        self.wt(t, ("p3s1", g - 2))
                    mm = t.matmul(pb[g % 2][:],
                                  kslice(R0, R0 + 128),
                                  qh[h % 2][b0:b0 + 64, L0:L0 + TTW],
                                  start=True, stop=True)
                    self.I(PE, mm)
                    self.mark(("p3qk", g), PE)

                @blk.vector
                def _(v):
                    self.wt(v, ("p3qk", g))
                    if lg == 0:
                        self.wt(v, ("p3rd", h * NTC + rc))
                        self.wt(v, ("p3rdq", h * NTC + rc))
                        if rc >= 2:
                            v.wait_ge(self.SEX.h, h * 96 + 80 + rc - 1)
                        elif h >= 1:
                            v.wait_ge(self.SEX.h,
                                      (h - 1) * 96 + 80 + rc + 15)
                    self.OP(v, VE, lambda: v.tensor_tensor(
                        s2f[:, (g % 8) * TTW:(g % 8 + 1) * TTW],
                        pb[g % 2][:], rk[rc % NR][:, L0:L0 + TTW], ALU.add))
                    self.mark(("p3s1", g), VE)

            half = (rc % 2) * 4 * TTW
            @blk.vector
            def _(v):
                self.OP(v, VE, lambda: v.tensor_tensor(
                    s2f[:, half:half + 4 * TTW],
                    s2f[:, half:half + 4 * TTW], rq[rc % NR][:], ALU.add))
                for lg in range(NLG):
                    self.mark(("p3s2", (h * NTC + rc) * NLG + lg), VE)

            if rc == 0:
                g0 = (h * NTC) * NLG
                @blk.tensor
                def _(t):
                    self.wt(t, ("p3s2", g0))
                    self.wt(t, "p3ones")
                    mm = t.matmul(pb[6][0:1, :], self.onesb[:],
                                  s2f[:, 0:TTW], start=True, stop=True)
                    self.I(PE, mm)
                    self.mark(("p3cs", h), PE)

                @blk.vector
                def _(v):
                    self.wt(v, ("p3cs", h))
                    self.OP(v, VE, lambda: v.reduce_sum(
                        c8[:, 0:1], pb[6][0:1, :], axis=AXX))
                    self.OP(v, VE, lambda: v.tensor_scalar(
                        c8[:, 1:2], c8[:, 0:1],
                        -1.0 / (128.0 * TTW * 8.0), None, ALU.mult))
                    self.mark(("p3c8", h), VE)

                @blk.tensor
                def _(t):
                    self.wt(t, ("p3c8", h))
                    mm = t.matmul(pb[6][:, 0:1], self.onesr[:],
                                  c8[:, 1:2], start=True, stop=True)
                    self.I(PE, mm)
                    self.mark(("p3cbm", h), PE)

                @blk.vector
                def _(v):
                    self.wt(v, ("p3cbm", h))
                    self.OP(v, VE, lambda: v.tensor_copy(
                        cb[:], pb[6][:, 0:1]))
                    self.mark(("p3cb", h), VE)

            self.megax.append((h, rc, half))

            @blk.tensor
            def _(t):
                t.wait_ge(self.SEX.h, h * 96 + 80 + rc + 1)
                for lg in range(NLG):
                    g = (h * NTC + rc) * NLG + lg
                    mm = t.matmul(pb[2 + lg][0:65, :],
                                  self.VT[:, (rc * NH + h) * 65:
                                          (rc * NH + h) * 65 + 65],
                                  prb[:, half + lg * TTW:
                                      half + (lg + 1) * TTW],
                                  start=(rc == 0), stop=(rc == NTC - 1))
                    self.I(PE, mm)
                    self.mark(("p3ctx", g), PE)

        SEX = self.SEX
        qxs2 = self.p3qxs
        pbL = self.pb
        @blk.scalar
        def _(s):
            pwl = [512, 512, 512, 512, 127]
            for (w2, k2, pg2) in self.megaq:
                if w2 >= 2:
                    self.wt(s, ("p3gw", "q", w2 - 2))
                self.wt(s, ("p3gm", h, pg2))
                if SEX.n:
                    s.wait_ge(SEX.h, SEX.n)
                ins = s.activation(
                    qxs2[w2 % 2][:, k2 * 512: k2 * 512 + pwl[k2]],
                    pbL[2 + pg2 % 6][:, 0:pwl[k2]], AF.Copy)
                SEX.n += 1
                ins.then_inc(SEX.h, 1)
            self.megaq = []
            self.wt(s, ("p3cb", h))
            for (hh2, rc2, half2) in self.megax:
                self.wt(s, ("p3s2", (hh2 * NTC + rc2) * NLG + 3))
                if ("p3ctx", (hh2 * NTC + rc2 - 2) * NLG + 3) in self.M:
                    self.wt(s, ("p3ctx", (hh2 * NTC + rc2 - 2) * NLG + 3))
                if SEX.n:
                    s.wait_ge(SEX.h, SEX.n)
                ins = s.activation(prb[:, half2:half2 + 4 * TTW],
                                   s2f[:, half2:half2 + 4 * TTW], AF.Exp,
                                   bias=cb[:], scale=0.125)
                SEX.n += 1
                ins.then_inc(SEX.h, 1)
            self.megax = []
        assert SEX.n == (h + 1) * 96, (h, SEX.n)

        for lg in range(NLG):
            @blk.vector
            def _(v):
                self.wt(v, ("p3ctx", (h * NTC + NTC - 1) * NLG + lg))
                self.OP(v, VE, lambda: v.reciprocal(denr[:],
                                                    pb[2 + lg][64:65, :]))
                self.mark(("p3denr", h, lg), VE)

            @blk.tensor
            def _(t):
                self.wt(t, ("p3denr", h, lg))
                mm = t.matmul(pb[6][0:DH, :], self.onesr[:, 0:DH], denr[:],
                              start=True, stop=True)
                self.I(PE, mm)
                self.mark(("p3denb", h, lg), PE)

            @blk.vector
            def _(v):
                self.wt(v, ("p3denb", h, lg))
                self.OP(v, VE, lambda: v.tensor_copy(denb[:],
                                                     pb[6][0:DH, :]))
                self.mark(("p3dene", h, lg), VE)
                if (h, lg) != (0, 0):
                    prev = (h, lg - 1) if lg >= 1 else (h - 1, NLG - 1)
                    self.wt(v, ("p3cxw",) + prev)
                self.OP(v, VE, lambda: v.tensor_tensor(
                    ctxo[:], pb[2 + lg][0:DH, :], denb[:], ALU.mult))
                self.mark(("p3ctxo", h, lg), VE)

            @blk.sync
            def _(sy):
                self.wt(sy, ("p3ctxo", h, lg))
                r0 = (h % 2) * 64
                self.D(SC, sy.dma_start(
                    out=self.scr["cx"].ap()[h // 2, r0:r0 + 64,
                                            lg * TTW:(lg + 1) * TTW],
                    in_=ctxo[:]))
                self.mark(("p3cxw", h, lg), SC)
        self.mark(("p3qdone", h), VE)

    def p3_oproj(self):
        blk = self.block
        i = self.inp
        LD, SX, SC, PE, VE, SE = (self.LD, self.SX, self.SC, self.PE,
                                  self.VE, self.SE)
        st = ExitStack()
        cxt = self.sb("cxt", [128, NDC * TTW], BF16, st)
        xnt2 = [self.sb(f"xnt2{k}", [128, TTW], F32, st) for k in range(2)]
        hset = self.sb("hset", [128, NDC * TTW], F32, st)
        hb = [self.sb(f"hb{k}", [128, TTW], BF16, st) for k in range(2)]
        rws = self.sb("rws3", [1, 3 * TTW], F32, st)
        sstg = [self.sb(f"sstg{k}", [128, TTW], F32, st) for k in range(2)]
        wob = self.sb("wob", [128, NDC * DM], BF16, st)
        MU = rws[:, 0:TTW]; TMP = rws[:, TTW:2 * TTW]
        BB = rws[:, 2 * TTW:3 * TTW]
        pb = self.pb

        @blk.gpsimd
        def _(g):
            for dc in range(NDC):
                self.D(LD, g.dma_start(
                    out=wob[:, dc * DM:(dc + 1) * DM],
                    in_=i["Wo"].ap()[dc * 128:(dc + 1) * 128, :]))
        self.mark("p3wo", LD)

        for tt in range(NTT):
            @blk.sync
            def _(sy):
                if tt >= 1:
                    self.wt(sy, ("p3om", tt - 1, NDC - 1))
                for dc in range(NDC):
                    self.D(SX, sy.dma_start(
                        out=cxt[:, dc * TTW:(dc + 1) * TTW],
                        in_=self.scr["cx"].ap()[dc, :,
                                                tt * TTW:(tt + 1) * TTW]))
                self.mark(("p3cxr", tt), SX)

            for half in range(2):
                @blk.tensor
                def _(t):
                    if tt == 0 and half == 0:
                        self.wt(t, "p3wo")
                    self.wt(t, ("p3cxr", tt))
                    if (tt, half) != (0, 0):
                        ph = (tt, half * 4 - 1) if half == 1 else                              (tt - 1, NDC - 1)
                        self.wt(t, ("p3he",) + ph)
                    for dc in range(NDC):
                        for o4 in range(4):
                            oc = half * 4 + o4
                            mm = t.matmul(
                                pb[o4][:],
                                wob[:, dc * DM + oc * 128:
                                    dc * DM + (oc + 1) * 128],
                                cxt[:, dc * TTW:(dc + 1) * TTW],
                                start=(dc == 0), stop=(dc == NDC - 1))
                            if dc == NDC - 1:
                                self.I(PE, mm)
                                self.mark(("p3om", tt, oc), PE)

                for o4 in range(4):
                    oc = half * 4 + o4
                    gidx = tt * NDC + oc

                    @blk.sync
                    def _(sy):
                        if gidx >= 2:
                            self.wt(sy, ("p3hx", gidx - 2))
                        self.D(SX, sy.dma_start(
                            out=xnt2[oc % 2][:],
                            in_=self.scr["xn"].ap()[oc, :,
                                                    tt * TTW:(tt + 1) * TTW]))
                        self.mark(("p3xnr", gidx), SX)

                    @blk.scalar
                    def _(s):
                        self.wt(s, ("p3om", tt, oc))
                        self.OP(s, SE, lambda: s.activation(
                            hset[:, oc * TTW:(oc + 1) * TTW], pb[o4][:],
                            AF.Identity, bias=self.pcol("bo", oc)))
                        self.mark(("p3he", tt, oc), SE)

                    @blk.vector
                    def _(v):
                        self.wt(v, ("p3he", tt, oc))
                        self.wt(v, ("p3xnr", gidx))
                        hd = hset[:, oc * TTW:(oc + 1) * TTW]
                        self.OP(v, VE, lambda: v.tensor_tensor(
                            hd, hd, xnt2[oc % 2][:], ALU.add))
                        self.mark(("p3hx", gidx), VE)
                        if oc >= 1:
                            self.wt(v, ("p3hst", tt, oc - 1))
                            self.wt(v, ("p3hsq", tt, oc - 1))
                        self.OP(v, VE, lambda: v.tensor_copy(hb[0][:], hd))
                        self.mark(("p3hb", tt, oc), VE)

                    @blk.tensor
                    def _(t):
                        self.wt(t, ("p3hb", tt, oc))
                        if tt >= 1 and oc == 0:
                            self.wt(t, ("p3rows", tt - 1))
                        mm = t.matmul(pb[4][0:1, :], self.onesb[:],
                                      hb[0][:], start=(oc == 0),
                                      stop=(oc == NDC - 1))
                        self.I(PE, mm)
                        self.mark(("p3hst", tt, oc), PE)

                    @blk.scalar
                    def _(s):
                        self.wt(s, ("p3hb", tt, oc))
                        if oc >= 1:
                            self.wt(s, ("p3hsq2", tt, oc - 1))
                        self.OP(s, SE, lambda: s.activation(hb[1][:],
                                                            hb[0][:],
                                                            AF.Square))
                        self.mark(("p3hsq", tt, oc), SE)

                    @blk.tensor
                    def _(t):
                        self.wt(t, ("p3hsq", tt, oc))
                        mm = t.matmul(pb[5][0:1, :], self.onesb[:],
                                      hb[1][:], start=(oc == 0),
                                      stop=(oc == NDC - 1))
                        self.I(PE, mm)
                        self.mark(("p3hsq2", tt, oc), PE)
            self.mark(("p3lnst", tt), PE)

            @blk.vector
            def _(v):
                self.wt(v, ("p3lnst", tt))
                self.OP(v, VE, lambda: v.tensor_scalar(
                    MU, pb[4][0:1, :], 1.0 / DM, None, ALU.mult))
                self.OP(v, VE, lambda: v.tensor_scalar(
                    TMP, pb[5][0:1, :], 1.0 / DM, None, ALU.mult))
                self.OP(v, VE, lambda: v.tensor_tensor(BB, MU, MU, ALU.mult))
                self.OP(v, VE, lambda: v.scalar_tensor_tensor(
                    TMP, BB, -1.0, TMP, ALU.mult, ALU.add))
                self.mark(("p3var", tt), VE)

            @blk.scalar
            def _(s):
                self.wt(s, ("p3var", tt))
                self.OP(s, SE, lambda: s.activation(
                    TMP, TMP, AF.Sqrt, bias=self.cst[0:1, 1:2]))
                self.OP(s, SE, lambda: s.activation(
                    self.cstj[0:1, 0:1], self.cst[0:1, 0:1], AF.Copy))
                self.mark(("p3sd", tt), SE)

            @blk.vector
            def _(v):
                self.wt(v, ("p3sd", tt))
                self.OP(v, VE, lambda: v.reciprocal(TMP, TMP))
                self.OP(v, VE, lambda: v.scalar_tensor_tensor(
                    BB, MU, -1.0, TMP, ALU.mult, ALU.mult))
                self.mark(("p3rows", tt), VE)

            @blk.tensor
            def _(t):
                self.wt(t, ("p3rows", tt))
                if tt >= 1:
                    self.wt(t, ("p3s", tt - 1, NDC - 1))
                self.I(PE, t.matmul(pb[6][:], self.onesr[:], TMP,
                                    start=True, stop=True))
                self.I(PE, t.matmul(pb[7][:], self.onesr[:], BB,
                                    start=True, stop=True))
                self.mark(("p3bc", tt), PE)

            for oc in range(NDC):
                gidx = tt * NDC + oc

                @blk.vector
                def _(v):
                    if oc == 0:
                        self.wt(v, ("p3bc", tt))
                    hd = hset[:, oc * TTW:(oc + 1) * TTW]
                    k = oc % 2
                    if gidx >= 2:
                        self.wt(v, ("p3sw", gidx - 2))
                    self.OP(v, VE, lambda: v.tensor_tensor(
                        sstg[k][:], hd, pb[6][:], ALU.mult))
                    self.OP(v, VE, lambda: v.tensor_tensor(
                        sstg[k][:], sstg[k][:], pb[7][:], ALU.add))
                    self.OP(v, VE, lambda: v.tensor_scalar(
                        sstg[k][:], sstg[k][:], self.pcol("lng", oc),
                        self.pcol("lnb", oc), ALU.mult, ALU.add))
                    self.mark(("p3s", tt, oc), VE)

                @blk.sync
                def _(sy):
                    self.wt(sy, ("p3s", tt, oc))
                    self.D(SC, sy.dma_start(
                        out=self.scr["sa"].ap()[oc, :,
                                                tt * TTW:(tt + 1) * TTW],
                        in_=sstg[oc % 2][:]))
                    self.mark(("p3sw", gidx), SC)

        self.barrier()
        st.close()

    def p4_moe(self):
        blk = self.block
        i = self.inp
        LD, SX, SC, PE, VE, SE = (self.LD, self.SX, self.SC, self.PE,
                                  self.VE, self.SE)
        self.stP4 = ExitStack()
        self.oacc = self.sb("oacc", [128, NDC * S], F32, self.stP4)
        st = self.stP4b = ExitStack()
        hh = self.sb("hh", [128, 4 * S], BF16, st)
        w1b = [self.sb(f"w1b{k}", [128, 8 * TTW], BF16, st) for k in range(2)]
        w2b = [self.sb(f"w2b{k}", [128, 4 * DM], BF16, st) for k in range(2)]
        cbc = self.sb("cbc", [128, S], F32, st)
        htmp = [self.sb(f"htmp{k}", [128, TTW], F32, st) for k in range(4)]
        b2s = self.sb("b2s", [NE, DM], F32, st)
        sel8 = self.sb("sel8_s", [NE, NE * 128], F32, st)
        pb = self.pb
        LD0 = LD.n

        @blk.gpsimd
        def _(g):
            self.D(LD, g.dma_start(out=b2s[:], in_=i["b2"].ap()))
            self.D(LD, g.dma_start(out=sel8[:], in_=self.sel8_h.ap()))

        def stream_block(blkid):
            e, fw = divmod(blkid, NDC)
            @blk.gpsimd
            def _(g):
                if blkid >= 2:
                    self.wt(g, ("p4m1l", blkid - 2))
                    self.wt(g, ("p4m2l", blkid - 2))
                for dc in range(NDC):
                    self.D(LD, g.dma_start(
                        out=w1b[blkid % 2][:, dc * TTW:(dc + 1) * TTW],
                        in_=i["w1"].ap()[e, dc * 128:(dc + 1) * 128,
                                         fw * TTW:(fw + 1) * TTW]))
                for fsub in range(4):
                    self.D(LD, g.dma_start(
                        out=w2b[blkid % 2][:, fsub * DM:(fsub + 1) * DM],
                        in_=i["w2"].ap()[e, fw * TTW + fsub * 128:
                                         fw * TTW + (fsub + 1) * 128, :]))

        stream_block(0)
        stream_block(1)

        def ldgate(blkid):
            return LD0 + 32 + 16 * 12 * (blkid + 1)

        for dc in range(NDC):
            for tt in range(NTT):
                j = dc * NTT + tt
                @blk.tensor
                def _(t):
                    if dc == 0 and tt == 0:
                        t.wait_ge(LD.h, LD0 + 32)
                    if j >= 2:
                        self.wt(t, ("p4bi", j - 2))
                    mm = t.matmul(pb[j % 2][:],
                                  b2s[:, dc * 128:(dc + 1) * 128],
                                  self.comb[:, tt * TTW:(tt + 1) * TTW],
                                  start=True, stop=True)
                    self.I(PE, mm)
                    self.mark(("p4bm", j), PE)

                @blk.vector
                def _(v):
                    self.wt(v, ("p4bm", j))
                    self.OP(v, VE, lambda: v.tensor_copy(
                        self.oacc[:, dc * S + tt * TTW:
                                  dc * S + (tt + 1) * TTW], pb[j % 2][:]))
                    self.mark(("p4bi", j), VE)
        self.mark("p4binit", VE)

        for e in range(NE):
            for tt in range(NTT):
                @blk.tensor
                def _(t):
                    if e == 0 and tt == 0:
                        self.wt(t, "p4binit")
                    if e >= 1 and tt == 0:
                        self.wt(t, ("p4m2e", e * NDC - 1, NDC - 1))
                    if tt >= 2:
                        self.wt(t, ("p4cbe", e, tt - 2))
                    mm = t.matmul(pb[tt % 2][:],
                                  sel8[:, e * 128:(e + 1) * 128],
                                  self.comb[:, tt * TTW:(tt + 1) * TTW],
                                  start=True, stop=True)
                    self.I(PE, mm)
                    self.mark(("p4cbm", e, tt), PE)

                @blk.vector
                def _(v):
                    self.wt(v, ("p4cbm", e, tt))
                    self.OP(v, VE, lambda: v.tensor_copy(
                        cbc[:, tt * TTW:(tt + 1) * TTW], pb[tt % 2][:]))
                    self.mark(("p4cbe", e, tt), VE)

            for fw in range(NDC):
                blkid = e * NDC + fw
                for fsub in range(4):
                    bb = (fsub % 2) * 4
                    @blk.tensor
                    def _(t):
                        t.wait_ge(LD.h, ldgate(blkid))
                        if fsub == 0 and fw == 0:
                            self.wt(t, ("p4cbe", e, NTT - 1))
                        if fsub >= 2:
                            self.wt(t, ("p4hh", blkid, fsub - 2))
                        elif blkid >= 1:
                            self.wt(t, ("p4m2e", blkid - 1, NDC - 1))
                        for dc in range(NDC):
                            lhsT = w1b[blkid % 2][
                                :, dc * TTW + fsub * 128:
                                dc * TTW + (fsub + 1) * 128]
                            for tt in range(NTT):
                                mm = t.matmul(
                                    pb[bb + tt][:], lhsT,
                                    self.n2b[:, dc * S + tt * TTW:
                                             dc * S + (tt + 1) * TTW],
                                    start=(dc == 0), stop=(dc == NDC - 1))
                        self.I(PE, mm)
                        self.mark(("p4m1", blkid, fsub), PE)
                        if fsub == 3:
                            self.mark(("p4m1l", blkid), PE)

                    @blk.scalar
                    def _(s):
                        self.wt(s, ("p4m1", blkid, fsub))
                        prevk = ((blkid, fsub - 1) if fsub >= 1
                                 else (blkid - 1, 3))
                        if ("p4hh",) + prevk in self.M:
                            self.wt(s, ("p4hh",) + prevk)
                        for tt in range(NTT):
                            self.OP(s, SE, lambda: s.activation(
                                htmp[tt][:], pb[bb + tt][:], AF.Silu,
                                bias=self.b1perp[:, e * 32 + fw * 4 + fsub:
                                                 e * 32 + fw * 4 + fsub + 1]))
                            self.mark(("p4sl", blkid, fsub, tt), SE)

                    @blk.vector
                    def _(v):
                        for tt in range(NTT):
                            self.wt(v, ("p4sl", blkid, fsub, tt))
                            self.OP(v, VE, lambda: v.tensor_tensor(
                                hh[:, fsub * S + tt * TTW:
                                   fsub * S + (tt + 1) * TTW],
                                htmp[tt][:],
                                cbc[:, tt * TTW:(tt + 1) * TTW], ALU.mult))
                            self.mark(("p4hhm", blkid, fsub, tt), VE)
                    self.mark(("p4hh", blkid, fsub), VE)

                for dcout in range(NDC):
                    bb = (dcout % 2) * 4
                    @blk.tensor
                    def _(t):
                        if dcout == 0:
                            self.wt(t, ("p4hh", blkid, 3))
                        if dcout >= 2:
                            self.wt(t, ("p4m2e", blkid, dcout - 2))
                        for fsub in range(4):
                            lhsT = w2b[blkid % 2][
                                :, fsub * DM + dcout * 128:
                                fsub * DM + (dcout + 1) * 128]
                            for tt in range(NTT):
                                mm = t.matmul(
                                    pb[bb + tt][:], lhsT,
                                    hh[:, fsub * S + tt * TTW:
                                       fsub * S + (tt + 1) * TTW],
                                    start=(fsub == 0), stop=(fsub == 3))
                        self.I(PE, mm)
                        self.mark(("p4m2", blkid, dcout), PE)
                        if dcout == NDC - 1:
                            self.mark(("p4m2l", blkid), PE)

                    @blk.vector
                    def _(v):
                        self.wt(v, ("p4m2", blkid, dcout))
                        for tt in range(NTT):
                            a = self.oacc[:, dcout * S + tt * TTW:
                                          dcout * S + (tt + 1) * TTW]
                            self.OP(v, VE, lambda: v.tensor_tensor(
                                a, a, pb[bb + tt][:], ALU.add))
                        self.mark(("p4m2e", blkid, dcout), VE)

                if blkid + 2 < NE * NDC:
                    stream_block(blkid + 2)

        self.barrier()

    def p5_final(self):
        blk = self.block
        i = self.inp
        LD, SX, SC, PE, VE, SE, ST = (self.LD, self.SX, self.SC, self.PE,
                                      self.VE, self.SE, self.ST)
        st = ExitStack()
        x2t = self.sb("x2t", [128, NDC * TTW], F32, st)
        prod = self.sb("prod", [128, NDC * TTW], F32, st)
        gst = [self.sb(f"gst{k}", [128, TTW], F32, st) for k in range(2)]
        sst = [self.sb(f"sst{k}", [128, TTW], F32, st) for k in range(2)]
        xtk = [self.sb(f"xtk{k}", [128, DM], F32, st) for k in range(2)]
        ostg = [self.sb(f"ostg{k}", [128, DM], F32, st) for k in range(2)]
        pb = self.pb

        for tt in range(NTT):
            @blk.sync
            def _(sy):
                if tt >= 1:
                    self.wt(sy, ("p5ff", tt - 1, NDC - 1))
                for dc in range(NDC):
                    self.D(SX, sy.dma_start(
                        out=x2t[:, dc * TTW:(dc + 1) * TTW],
                        in_=self.scr["x2"].ap()[dc, :,
                                                tt * TTW:(tt + 1) * TTW]))
                self.mark(("p5x2", tt), SX)

            for dc in range(NDC):
                j = tt * NDC + dc

                @blk.sync
                def _(sy):
                    if j >= 2:
                        self.wt(sy, ("p5pr", j - 2))
                    self.D(SC, sy.dma_start(
                        out=sst[dc % 2][:],
                        in_=self.scr["sa"].ap()[dc, :,
                                                tt * TTW:(tt + 1) * TTW]))
                    self.mark(("p5sr", j), SX if False else SC)

                @blk.vector
                def _(v):
                    if dc == 0:
                        self.wt(v, ("p5x2", tt))
                    xd = x2t[:, dc * TTW:(dc + 1) * TTW]
                    if j >= 2:
                        self.wt(v, ("p5g", j - 2))
                    self.OP(v, VE, lambda: v.tensor_tensor(
                        gst[dc % 2][:], xd,
                        self.oacc[:, dc * S + tt * TTW:
                                  dc * S + (tt + 1) * TTW], ALU.add))
                    self.mark(("p5ff", tt, dc), VE)

                @blk.scalar
                def _(s):
                    self.wt(s, ("p5ff", tt, dc))
                    self.OP(s, SE, lambda: s.activation(
                        gst[dc % 2][:], gst[dc % 2][:], AF.Silu))
                    self.mark(("p5g", j), SE)

                @blk.vector
                def _(v):
                    self.wt(v, ("p5g", j))
                    self.wt(v, ("p5sr", j))
                    if tt >= 1:
                        self.wt(v, ("p5tp", tt - 1, 3))
                    self.OP(v, VE, lambda: v.tensor_tensor(
                        prod[:, dc * TTW:(dc + 1) * TTW], gst[dc % 2][:],
                        sst[dc % 2][:], ALU.mult))
                    self.mark(("p5pr", j), VE)

            for tsub in range(4):
                tc = tt * 4 + tsub
                @blk.sync
                def _(sy):
                    if tc >= 2:
                        self.wt(sy, ("p5os", tc - 2))
                    self.D(SX, sy.dma_start(
                        out=xtk[tsub % 2][:],
                        in_=i["x"].ap()[tc * 128:(tc + 1) * 128, :]))
                    self.mark(("p5xr", tc), SX)

                @blk.tensor
                def _(t):
                    self.wt(t, ("p5pr", tt * NDC + NDC - 1))
                    if tc >= 2:
                        self.wt(t, ("p5oe", tc - 2))
                    bb = (tsub % 2) * 2
                    for dc in range(NDC):
                        mm = t.transpose(
                            pb[bb + dc // 4][:, (dc % 4) * 128:
                                             (dc % 4 + 1) * 128],
                            prod[:, dc * TTW + tsub * 128:
                                 dc * TTW + (tsub + 1) * 128],
                            self.eye[:])
                        if dc % 4 == 3:
                            self.I(PE, mm)
                    self.mark(("p5tp", tt, tsub), PE)

                @blk.vector
                def _(v):
                    self.wt(v, ("p5tp", tt, tsub))
                    self.wt(v, ("p5xr", tc))
                    bb = (tsub % 2) * 2
                    for half in range(2):
                        self.OP(v, VE, lambda: v.tensor_tensor(
                            ostg[tsub % 2][:, half * 512:(half + 1) * 512],
                            pb[bb + half][:],
                            xtk[tsub % 2][:, half * 512:(half + 1) * 512],
                            ALU.add))
                    self.mark(("p5oe", tc), VE)

                @blk.sync
                def _(sy):
                    self.wt(sy, ("p5oe", tc))
                    self.D(ST, sy.dma_start(
                        out=self.out_h.ap()[tc * 128:(tc + 1) * 128, :],
                        in_=ostg[tsub % 2][:]))
                    self.mark(("p5os", tc), ST)

        @blk.sync
        def _(sy):
            self.wait_all(sy)
            sy.wait_ge(ST.h, ST.n)
        st.close()

    def finish_dev(self):
        blk = self.block
        ST = self.ST
        dumps = []
        for name in DEV_TAPS:
            t = getattr(self, {"xnb": "xnb", "K_sb": "K_sb", "VT": "VT",
                               "n2b": "n2b", "comb": "comb"}.get(name, "nope"),
                        None)
            if t is not None:
                h = self.nc.dram_tensor("dump_" + name, list(t.ap().shape),
                                        t.ap().dtype, kind="ExternalOutput")
                dumps.append((h, t))

        @blk.sync
        def _(sy):
            self.wait_all(sy)
            for h, t in dumps:
                self.D(ST, sy.dma_start(out=h.ap(), in_=t[:]))
            self.D(ST, sy.dma_start(out=self.out_h.ap()[0:128, 0:128],
                                    in_=self.eye[:]))
            sy.wait_ge(ST.h, ST.n)


def build(dev_stop=None):
    b = Builder()
    with b.ctx:
        def done():
            b.finish_dev()
            for name in ("stP4b", "stP4", "stP3", "stP2", "stBV"):
                stx = getattr(b, name, None)
                if stx is not None:
                    stx.close()
            for name in ("stXNB", "stW0", "stW", "stKV"):
                getattr(b, name).close()
            return b

        b.declare()
        b.consts()
        b.p0_preln()
        if dev_stop == "p0":
            return done()
        b.p1_qkv()
        if dev_stop == "p1":
            return done()
        b.stBV.close()
        b.p2_front()
        if dev_stop == "p2":
            return done()
        b.stXNB.close()
        b.stW0.close()
        b.stW.close()
        b.p3_attn()
        b.stP3.close()
        b.p3_oproj()
        if dev_stop == "p3":
            return done()
        b.stKV.close()
        b.p4_moe()
        if dev_stop == "p4":
            return done()
        b.stP4b.close()
        b.p5_final()
        b.stP4.close()
        b.stP2.close()
        return b


def make_dex(dist_emb):
    M = 1024
    NDE = 2 * M - 1
    u = np.arange(UW)
    g = np.where(u <= M - 1, NDE - 1,
                 np.where(u <= 3 * M - 2, (3 * M - 2) - u, (5 * M - 3) - u))
    g = np.clip(g, 0, NDE - 1)
    g2 = np.where(u <= M - 1, u + (M - 1),
                  np.where(u <= 3 * M - 2, u - M,
                           np.full_like(u, NDE - 1)))
    g2 = np.clip(g2, 0, NDE - 1)
    dexq = np.ascontiguousarray(dist_emb[g].T)
    dexk = np.ascontiguousarray(dist_emb[g2].T)
    return dexq, dexk


def kernel(**inputs):
    x = np.ascontiguousarray(np.asarray(inputs["L0_attn_hidden_states"],
                                        dtype=np.float32))
    dexq, dexk = make_dex(np.asarray(inputs["L0_attn_dist_emb"],
                                     dtype=np.float32))
    shared = {
        "Wq": inputs["L0_attn_Wq"], "Wk": inputs["L0_attn_Wk"],
        "Wv": inputs["L0_attn_Wv"], "Wo": inputs["L0_attn_Wo"],
        "bq": inputs["L0_attn_bq"], "bk": inputs["L0_attn_bk"],
        "bv": inputs["L0_attn_bv"], "bo": inputs["L0_attn_bo"],
        "lng": inputs["L0_attn_ln_gamma"], "lnb": inputs["L0_attn_ln_beta"],
        "g1": inputs["L0_ffn_g1"], "be1": inputs["L0_ffn_be1"],
        "g2": inputs["L0_ffn_g2"], "be2": inputs["L0_ffn_be2"],
        "mixw": inputs["L0_ffn_mix_w"], "mixb": inputs["L0_ffn_mix_b"],
        "rw": inputs["L0_ffn_router_w"],
        "w1": inputs["L0_ffn_w1"], "b1": inputs["L0_ffn_b1"],
        "w2": inputs["L0_ffn_w2"], "b2": inputs["L0_ffn_b2"],
    }
    shared = {k: np.ascontiguousarray(np.asarray(v, dtype=np.float32))
              for k, v in shared.items()}
    shared["dexq"] = dexq
    shared["dexk"] = dexk
    b = build(DEV_STOP)
    ncores = x.shape[0]
    in_maps = []
    for c in range(ncores):
        m = dict(shared)
        m["x"] = np.ascontiguousarray(x[c])
        in_maps.append(m)
    res = run_bass_kernel_spmd(b.nc, in_maps, core_ids=list(range(ncores)))
    out = np.stack([r["out"] for r in res.results], axis=0)
    if DEV_TAPS:
        kernel.taps = [dict(r) for r in res.results]
    return out


# version 07 (final)
# speedup vs baseline: 1.0167x; 1.0167x over previous
import numpy as np
import ml_dtypes
import concourse.bass as bass
import concourse.mybir as mybir
from concourse.bass_utils import run_bass_kernel_spmd
from concourse.ap import AP
from contextlib import ExitStack

AF = mybir.ActivationFunctionType
ALU = mybir.AluOpType
AXX = mybir.AxisListType.X
F32 = mybir.dt.float32
BF16 = mybir.dt.bfloat16
bfdt = ml_dtypes.bfloat16

S = 2048; DM = 1024; NDC = 8; NH = 16; DH = 64; NE = 8; FF = 4096
NTT = 4; TTW = 512; NTC = 16
UW = 4096; QW = 2560; KW = 2176; NLG = 4

DEV_TAPS = ()
DEV_STOP = None


class Sem:
    def __init__(self, nc, ctx, name):
        self.h = ctx.enter_context(nc.semaphore(name))
        self.n = 0


class Builder:
    def __init__(self):
        self.nc = bass.Bass("TRN2", target_bir_lowering=False)
        self.ctx = ExitStack()
        self.M = {}

    def sb(self, name, shape, dt=F32, stack=None, side=None):
        if side is None:
            side = "left" if stack is None else "right"
        return (stack or self.ctx).enter_context(
            self.nc.sbuf_tensor(name, shape, dt, side=side))

    def dram(self, name, shape, dt=F32):
        kind = "ExternalOutput" if name in DEV_TAPS else "Internal"
        return self.nc.dram_tensor(name, list(shape), dt, kind=kind)

    def I(self, sem, ins, k=1):
        sem.n += k
        ins.then_inc(sem.h, k)
        return sem.n

    def D(self, sem, ins):
        return self.I(sem, ins, 16)

    def OP(self, eng, sem, f):
        if sem.n:
            eng.wait_ge(sem.h, sem.n)
        ins = f()
        sem.n += 1
        ins.then_inc(sem.h, 1)
        return ins

    def mark(self, label, sem):
        self.M[label] = (sem, sem.n)

    def wt(self, eng, label):
        sem, n = self.M[label]
        if n > 0:
            eng.wait_ge(sem.h, n)

    def wait_all(self, eng):
        for s in self.all_sems:
            if s.n > 0:
                eng.wait_ge(s.h, s.n)

    def barrier(self):
        blk = self.block
        for deco in (blk.tensor, blk.scalar, blk.vector, blk.gpsimd, blk.sync):
            @deco
            def _(eng):
                self.wait_all(eng)

    def declare(self):
        nc = self.nc

        def din(name, shape):
            return nc.dram_tensor(name, list(shape), F32, kind="ExternalInput")
        i = self.inp = {}
        i["x"] = din("x", (S, DM))
        for n in ("Wq", "Wk", "Wv", "Wo", "mixw"):
            i[n] = din(n, (DM, DM))
        for n in ("bq", "bk", "bv", "bo", "lng", "lnb", "g1", "be1", "g2",
                  "be2", "mixb"):
            i[n] = din(n, (DM,))
        i["dexq"] = din("dexq", (DH, UW)); i["dexk"] = din("dexk", (DH, UW))
        i["rw"] = din("rw", (DM, NE))
        i["w1"] = din("w1", (NE, DM, FF)); i["b1"] = din("b1", (NE, FF))
        i["w2"] = din("w2", (NE, FF, DM)); i["b2"] = din("b2", (NE, DM))
        self.out_h = nc.dram_tensor("out", [S, DM], F32, kind="ExternalOutput")

        d = self.scr = {}
        d["xn"] = self.dram("xn", (NDC, 128, S))
        d["x2"] = self.dram("x2", (NDC, 128, S))
        d["sa"] = self.dram("sa", (NDC, 128, S))
        d["qs"] = self.dram("qs", (DM, S), BF16)
        d["cx"] = self.dram("cx", (NDC, 128, S), BF16)
        d["rq"] = self.dram("rq", (NH, S, 2304), BF16)
        d["rk"] = self.dram("rk", (NH, S, 2304), BF16)

        self.eye_h = nc.inline_tensor(np.eye(128, dtype=np.float32), "eyef")
        self.onesb_h = nc.inline_tensor(np.ones((128, 1), dtype=bfdt), "onespb")
        self.onesr_h = nc.inline_tensor(np.ones((1, 128), dtype=np.float32),
                                        "onesrf")
        sel_np = np.zeros((NE, NE * 128), dtype=np.float32)
        for e in range(NE):
            sel_np[e, e * 128:(e + 1) * 128] = 1.0
        self.sel8_h = nc.inline_tensor(sel_np, "sel8")

        self.eye = self.sb("eye", [128, 128], F32)
        self.onesb = self.sb("onesb", [128, 1], BF16)
        self.onesr = self.sb("onesr", [1, 128], F32)
        self.perp = self.sb("perp", [128, 8 * 11], F32)
        self.PERP = {n: k for k, n in enumerate(
            ["bq", "bk", "bv", "bo", "lng", "lnb", "g1", "be1", "g2", "be2",
             "mixb"])}
        self.b1perp = self.sb("b1perp", [128, NE * 32], F32)
        self.rw_sb = self.sb("rw_sb", [128, NDC * NE], F32)
        self.cst = self.sb("cst", [128, 2], F32)
        self.cstj = self.sb("cstj", [128, 2], F32)
        self.stBV = ExitStack()
        self.vstage = self.sb("vstage", [32, 128], F32, self.stBV)
        self.bvrow = self.sb("bvrow", [1, DM], F32, self.stBV)
        self.bvb = self.sb("bvb", [128, DM], F32, self.stBV)
        self.stKV = ExitStack()
        self.K_sb = self.sb("K_sb", [128, NDC * S], BF16, self.stKV,
                            side="left")
        self.VT = self.sb("VT", [128, NTC * NH * 65], BF16, self.stKV,
                          side="left")
        self.stW = ExitStack()
        wbuf1 = self.sb("wbuf1", [128, NDC * DM], BF16, self.stW,
                        side="left")
        self.stW0 = ExitStack()
        wbuf0 = self.sb("wbuf0", [128, NDC * DM], BF16, self.stW0,
                        side="left")
        self.wbuf = [wbuf0, wbuf1]
        self.stXNB = ExitStack()
        self.xnb = self.sb("xnb", [128, NDC * S], BF16, self.stXNB,
                           side="left")
        self.pb = [self.ctx.enter_context(
            self.nc.psum_tensor(f"pb{k}", [128, TTW], F32)) for k in range(8)]

        c = self.ctx
        self.LD = Sem(nc, c, "LD"); self.SX = Sem(nc, c, "SX")
        self.SC = Sem(nc, c, "SC"); self.PE = Sem(nc, c, "PE")
        self.VE = Sem(nc, c, "VE"); self.SE = Sem(nc, c, "SE")
        self.GP = Sem(nc, c, "GP"); self.ST = Sem(nc, c, "ST")
        self.SEX = Sem(nc, c, "SEX")
        self.SD = Sem(nc, c, "SD")
        self.all_sems = [self.LD, self.SX, self.SC, self.PE, self.VE,
                         self.SE, self.GP, self.ST, self.SEX, self.SD]
        self.block = self.ctx.enter_context(nc.Block())

    def pcol(self, name, oc):
        k = self.PERP[name]
        return self.perp[:, k * 8 + oc: k * 8 + oc + 1]

    def consts(self):
        blk = self.block
        i = self.inp
        LD, SX, PE, VE, SE = self.LD, self.SX, self.PE, self.VE, self.SE
        vstage = self.vstage
        bvrow = self.bvrow
        vecs = ["bq", "bk", "bv", "bo", "lng", "lnb", "g1", "be1", "g2",
                "be2", "mixb"]
        NV = len(vecs)
        LD0, SX0, PE0, VE0, SE0 = LD.n, SX.n, PE.n, VE.n, SE.n

        @blk.gpsimd
        def _(g):
            for dst, src in [(self.eye, self.eye_h), (self.onesb, self.onesb_h),
                             (self.onesr, self.onesr_h)]:
                self.D(LD, g.dma_start(out=dst[:], in_=src.ap()))
            for dc in range(NDC):
                self.D(LD, g.dma_start(
                    out=self.rw_sb[:, dc * NE:(dc + 1) * NE],
                    in_=AP(i["rw"], dc * 128 * NE, [[NE, 128], [1, NE]]),
                    allow_slow_non_contiguous=True))

        @blk.sync
        def _(sy):
            for k in range(NV):
                if k >= 1:
                    sy.wait_ge(PE.h, PE0 + k)
                self.D(SX, sy.dma_start(
                    out=vstage[0:8, :],
                    in_=AP(i[vecs[k]], 0, [[128, 8], [1, 128]])))
            for e in range(NE):
                sy.wait_ge(PE.h, PE0 + NV + e)
                self.D(SX, sy.dma_start(
                    out=vstage[:],
                    in_=AP(i["b1"], e * FF, [[128, 32], [1, 128]])))
            self.D(SX, sy.dma_start(
                out=bvrow[:], in_=AP(i["bv"], 0, [[DM, 1], [1, DM]])))

        @blk.tensor
        def _(t):
            t.wait_ge(LD.h, LD0 + 16 * 3)
            for k in range(NV):
                t.wait_ge(SX.h, SX0 + 16 * (k + 1))
                if k >= 2:
                    t.wait_ge(VE.h, VE0 + 2 + k - 1)
                self.I(PE, t.transpose(self.pb[k % 2][:, 0:8], vstage[0:8, :],
                                       self.eye[0:8, 0:8]))
            for e in range(NE):
                t.wait_ge(SX.h, SX0 + 16 * (NV + e + 1))
                if e >= 2:
                    t.wait_ge(VE.h, VE0 + 2 + NV + e - 1)
                self.I(PE, t.transpose(self.pb[(NV + e) % 2][:, 0:32],
                                       vstage[:], self.eye[0:32, 0:32]))
            t.wait_ge(SX.h, SX0 + 16 * (NV + NE + 1))
            for half in range(2):
                self.I(PE, t.matmul(self.pb[2 + half][:], self.onesr[:],
                                    bvrow[:, half * 512:(half + 1) * 512],
                                    start=True, stop=True))

        @blk.vector
        def _(v):
            self.OP(v, VE, lambda: v.memset(self.cst[:, 0:1], 1e-5))
            self.OP(v, VE, lambda: v.memset(self.cst[:, 1:2], 1e-12))
            for k in range(NV):
                v.wait_ge(PE.h, PE0 + k + 1)
                kk = self.PERP[vecs[k]]
                self.OP(v, VE, lambda: v.tensor_copy(
                    self.perp[:, kk * 8:(kk + 1) * 8], self.pb[k % 2][:, 0:8]))
            for e in range(NE):
                v.wait_ge(PE.h, PE0 + NV + e + 1)
                self.OP(v, VE, lambda: v.tensor_copy(
                    self.b1perp[:, e * 32:(e + 1) * 32],
                    self.pb[(NV + e) % 2][:, 0:32]))

        @blk.scalar
        def _(s):
            for half in range(2):
                s.wait_ge(PE.h, PE0 + NV + NE + half + 1)
                self.OP(s, SE, lambda: s.activation(
                    self.bvb[:, half * 512:(half + 1) * 512],
                    self.pb[2 + half][:], AF.Copy))

        self.barrier()

    def p0_preln(self):
        blk = self.block
        SX, SC, PE, VE, SE = self.SX, self.SC, self.PE, self.VE, self.SE
        st = ExitStack()
        xin = [self.sb(f"xin{k}", [128, DM], F32, st) for k in range(2)]
        xsq = self.sb("xsq", [128, DM], F32, st)
        xst = self.sb("xst", [128, 8], F32, st)
        xnt = self.sb("xnt", [128, DM], F32, st)
        stg = [self.sb(f"xnstg{k}", [128, DM], F32, st) for k in range(2)]
        SX0, SC0, PE0, VE0, SE0 = SX.n, SC.n, PE.n, VE.n, SE.n

        @blk.vector
        def _(v):
            def chain(tc):
                v.wait_ge(SX.h, SX0 + 16 * (tc + 1))
                b = xin[tc % 2]
                for f in [
                    lambda: v.reduce_sum(xst[:, 0:1], b[:], axis=AXX),
                    lambda: v.tensor_tensor(xsq[:], b[:], b[:], ALU.mult),
                    lambda: v.reduce_sum(xst[:, 1:2], xsq[:], axis=AXX),
                    lambda: v.tensor_scalar(xst[:, 2:3], xst[:, 0:1],
                                            1.0 / DM, None, ALU.mult),
                    lambda: v.tensor_scalar(xst[:, 3:4], xst[:, 1:2],
                                            1.0 / DM, None, ALU.mult),
                    lambda: v.tensor_tensor(xst[:, 4:5], xst[:, 2:3],
                                            xst[:, 2:3], ALU.mult),
                    lambda: v.scalar_tensor_tensor(xst[:, 5:6], xst[:, 4:5],
                                                   -1.0, xst[:, 3:4],
                                                   ALU.mult, ALU.add),
                ]:
                    self.OP(v, VE, f)
                self.mark(("p0var", tc), VE)
                v.wait_ge(SE.h, SE0 + 2 * (tc + 1))
                self.OP(v, VE, lambda: v.reciprocal(xst[:, 7:8], xst[:, 6:7]))
                if tc >= 1:
                    v.wait_ge(PE.h, PE0 + 2 * tc)
                self.OP(v, VE, lambda: v.tensor_scalar(
                    xnt[:], b[:], xst[:, 2:3], xst[:, 7:8],
                    ALU.subtract, ALU.mult))
                self.mark(("p0chain", tc), VE)

            def evac(tc):
                bankbase = 4 + 2 * (tc % 2)
                v.wait_ge(PE.h, PE0 + 2 * tc + 2)
                if tc >= 2:
                    v.wait_ge(SC.h, SC0 + 16 * (tc - 1))
                for half in range(2):
                    self.OP(v, VE, lambda: v.tensor_copy(
                        stg[tc % 2][:, half * 512:(half + 1) * 512],
                        self.pb[bankbase + half][:]))
                self.mark(("p0stg", tc), VE)
                for dc in range(NDC):
                    bank = bankbase + dc // 4
                    self.OP(v, VE, lambda: v.tensor_copy(
                        self.xnb[:, dc * S + tc * 128: dc * S + (tc + 1) * 128],
                        self.pb[bank][:, (dc % 4) * 128:(dc % 4 + 1) * 128]))
                self.mark(("p0evac", tc), VE)

            for tc in range(NTC):
                chain(tc)
                if tc >= 1:
                    evac(tc - 1)
            evac(NTC - 1)

        @blk.scalar
        def _(s):
            for tc in range(NTC):
                self.wt(s, ("p0var", tc))
                self.OP(s, SE, lambda: s.activation(xst[:, 6:7], xst[:, 5:6],
                                                    AF.Sqrt,
                                                    bias=self.cst[:, 0:1]))
                self.OP(s, SE, lambda: s.activation(xst[:, 4:5],
                                                    self.cst[:, 0:1], AF.Copy))

        @blk.tensor
        def _(t):
            for tc in range(NTC):
                self.wt(t, ("p0chain", tc))
                if tc >= 2:
                    self.wt(t, ("p0evac", tc - 2))
                bankbase = 4 + 2 * (tc % 2)
                for dc in range(NDC):
                    bank = bankbase + dc // 4
                    mm = t.transpose(
                        self.pb[bank][:, (dc % 4) * 128:(dc % 4 + 1) * 128],
                        xnt[:, dc * 128:(dc + 1) * 128], self.eye[:])
                    if dc % 4 == 3:
                        self.I(PE, mm)

        @blk.sync
        def _(sy):
            for tc in range(NTC + 1):
                if tc < NTC:
                    if tc >= 2:
                        self.wt(sy, ("p0chain", tc - 2))
                    self.D(SX, sy.dma_start(
                        out=xin[tc % 2][:],
                        in_=self.inp["x"].ap()[tc * 128:(tc + 1) * 128, :]))
                if tc >= 1:
                    self.wt(sy, ("p0stg", tc - 1))
                    dst = self.scr["xn"].ap().rearrange("a p t -> p a t")[
                        :, :, (tc - 1) * 128: tc * 128]
                    self.D(SC, sy.dma_start(
                        out=dst, in_=stg[(tc - 1) % 2][:].rearrange(
                            "p (a t) -> p a t", t=128)))

        self.barrier()
        st.close()

    def p1_qkv(self):
        blk = self.block
        i = self.inp
        LD, SC, PE, VE, SE = self.LD, self.SC, self.PE, self.VE, self.SE
        st = ExitStack()
        qstg = [self.sb(f"qstg{k}", [128, TTW], BF16, st) for k in range(2)]
        LD0, SC0, PE0, VE0, SE0 = LD.n, SC.n, PE.n, VE.n, SE.n

        @blk.gpsimd
        def _(g):
            for dc in range(NDC):
                self.D(LD, g.dma_start(
                    out=self.wbuf[0][:, dc * DM:(dc + 1) * DM],
                    in_=i["Wq"].ap()[dc * 128:(dc + 1) * 128, :]))
            for dc in range(NDC):
                self.D(LD, g.dma_start(
                    out=self.wbuf[1][:, dc * DM:(dc + 1) * DM],
                    in_=i["Wk"].ap()[dc * 128:(dc + 1) * 128, :]))
            g.wait_ge(PE.h, PE0 + 8)
            for dc in range(NDC):
                self.D(LD, g.dma_start(
                    out=self.wbuf[0][:, dc * DM:(dc + 1) * DM],
                    in_=i["Wv"].ap()[dc * 128:(dc + 1) * 128, :]))

        @blk.vector
        def _(v):
            ap = self.VT[:].rearrange("p (a c) -> p a c", c=65)[:, :, 64:65]
            self.OP(v, VE, lambda: v.memset(ap, 1.0))

        def proj_pe(t, wb, ldcount, se_gate):
            for oc in range(NDC):
                t.wait_ge(LD.h, LD0 + 16 * ldcount)
                if se_gate(oc) is not None:
                    t.wait_ge(SE.h, se_gate(oc))
                for dc in range(NDC):
                    lhsT = wb[:, dc * DM + oc * 128: dc * DM + (oc + 1) * 128]
                    for tt in range(NTT):
                        mm = t.matmul(
                            self.pb[(oc % 2) * 4 + tt][:], lhsT,
                            self.xnb[:, dc * S + tt * TTW:
                                     dc * S + (tt + 1) * TTW],
                            start=(dc == 0), stop=(dc == NDC - 1))
                self.I(PE, mm)

        @blk.tensor
        def _(t):
            proj_pe(t, self.wbuf[0], 8,
                    lambda oc: SE0 + 4 * (oc - 1) if oc >= 2 else None)

        @blk.scalar
        def _(s):
            for oc in range(NDC):
                s.wait_ge(PE.h, PE0 + oc + 1)
                for tt in range(NTT):
                    k = oc * NTT + tt
                    if k >= 2:
                        s.wait_ge(SC.h, SC0 + 16 * (k - 1))
                    self.OP(s, SE, lambda: s.activation(
                        qstg[k % 2][:], self.pb[(oc % 2) * 4 + tt][:],
                        AF.Identity, bias=self.pcol("bq", oc)))
                    s.wait_ge(SE.h, SE.n)
                    self.D(SC, s.dma_start(
                        out=self.scr["qs"].ap()[oc * 128:(oc + 1) * 128,
                                                tt * TTW:(tt + 1) * TTW],
                        in_=qstg[k % 2][:]))

        @blk.tensor
        def _(t):
            proj_pe(t, self.wbuf[1], 16,
                    lambda oc: (SE0 + 4 * (oc - 1) + 32 if oc >= 2
                                else SE0 + 32))

        @blk.scalar
        def _(s):
            for oc in range(NDC):
                s.wait_ge(PE.h, PE0 + 8 + oc + 1)
                for tt in range(NTT):
                    self.OP(s, SE, lambda: s.activation(
                        self.K_sb[:, oc * S + tt * TTW: oc * S + (tt + 1) * TTW],
                        self.pb[(oc % 2) * 4 + tt][:], AF.Identity,
                        bias=self.pcol("bk", oc)))

        PE_V0 = PE0 + 16
        @blk.tensor
        def _(t):
            t.wait_ge(LD.h, LD0 + 16 * 24)
            t.wait_ge(SE.h, SE0 + 64)
            for tc in range(NTC):
                if tc >= 2:
                    t.wait_ge(VE.h, VE0 + 1 + 2 * (tc - 1))
                for half in range(2):
                    for dc in range(NDC):
                        lhsT = self.xnb[:, dc * S + tc * 128:
                                        dc * S + (tc + 1) * 128]
                        mm = t.matmul(
                            self.pb[(tc % 2) * 2 + half][:], lhsT,
                            self.wbuf[0][:, dc * DM + half * 512:
                                         dc * DM + (half + 1) * 512],
                            start=(dc == 0), stop=(dc == NDC - 1))
                    self.I(PE, mm)

        @blk.vector
        def _(v):
            for tc in range(NTC):
                for half in range(2):
                    v.wait_ge(PE.h, PE_V0 + 2 * tc + half + 1)
                    src = self.pb[(tc % 2) * 2 + half][:].rearrange(
                        "p (a c) -> p a c", c=64)
                    d0 = (tc * NH + half * 8) * 65
                    dst = self.VT[:, d0: d0 + 8 * 65].rearrange(
                        "p (a c) -> p a c", c=65)[:, :, 0:64]
                    self.OP(v, VE, lambda: v.scalar_tensor_tensor(
                        dst, src, 1.0,
                        self.bvb[:, half * 512:(half + 1) * 512].rearrange(
                            "p (a c) -> p a c", c=64),
                        ALU.mult, ALU.add))

        self.barrier()
        st.close()

    def p2_front(self):
        blk = self.block
        i = self.inp
        LD, SX, SC, PE, VE, SE = (self.LD, self.SX, self.SC, self.PE,
                                  self.VE, self.SE)
        self.stP2 = ExitStack()
        self.n2b = self.sb("n2b", [128, NDC * S], BF16, self.stP2)
        self.comb = self.sb("comb", [NE, S], F32, self.stP2)
        st = ExitStack()
        xns = self.sb("xns", [128, NDC * TTW], F32, st)
        nb = self.sb("nb", [128, NDC * TTW], BF16, st)
        ntmp = self.sb("ntmp", [128, TTW], F32, st)
        n2t = self.sb("n2t", [128, TTW], F32, st)
        sqb = [self.sb(f"sqb{k}", [128, TTW], BF16, st) for k in range(2)]
        rws = self.sb("rws", [1, 3 * TTW], F32, st)
        rlog = ntmp[0:NE, :]
        lgT = n2t[:, 0:48]
        combsub = n2t[:, 48:80]
        smt = n2t[:, 80:120]
        LD0 = LD.n

        @blk.gpsimd
        def _(g):
            for dc in range(NDC):
                self.D(LD, g.dma_start(
                    out=self.wbuf[0][:, dc * DM:(dc + 1) * DM],
                    in_=i["mixw"].ap()[dc * 128:(dc + 1) * 128, :]))
        self.mark("p2mixw", LD)

        for tt in range(NTT):
            self._p2_tt(tt, xns, nb, ntmp, n2t, sqb, rws, rlog, lgT, smt,
                        combsub)

        self.barrier()
        st.close()

    def _p2_rows(self, tag, tt, rws, src0, src1):
        blk = self.block
        PE, VE, SE = self.PE, self.VE, self.SE
        MU = rws[:, 0:TTW]; TMP = rws[:, TTW:2 * TTW]
        BB = rws[:, 2 * TTW:3 * TTW]; JNK = self.cstj[0:1, 0:1]

        @blk.vector
        def _(v):
            self.wt(v, (tag + "st", tt))
            self.OP(v, VE, lambda: v.tensor_scalar(MU, src0, 1.0 / DM, None,
                                                   ALU.mult))
            self.OP(v, VE, lambda: v.tensor_scalar(TMP, src1, 1.0 / DM, None,
                                                   ALU.mult))
            self.OP(v, VE, lambda: v.tensor_tensor(BB, MU, MU, ALU.mult))
            self.OP(v, VE, lambda: v.scalar_tensor_tensor(
                TMP, BB, -1.0, TMP, ALU.mult, ALU.add))
            self.mark((tag + "var", tt), VE)

        @blk.scalar
        def _(s):
            self.wt(s, (tag + "var", tt))
            self.OP(s, SE, lambda: s.activation(TMP, TMP, AF.Sqrt,
                                                bias=self.cst[0:1, 0:1]))
            self.OP(s, SE, lambda: s.activation(JNK, self.cst[0:1, 0:1],
                                                AF.Copy))
            self.mark((tag + "sd", tt), SE)

        @blk.vector
        def _(v):
            self.wt(v, (tag + "sd", tt))
            self.OP(v, VE, lambda: v.reciprocal(TMP, TMP))
            self.OP(v, VE, lambda: v.scalar_tensor_tensor(
                BB, MU, -1.0, TMP, ALU.mult, ALU.mult))
            self.mark((tag + "rows", tt), VE)

        @blk.tensor
        def _(t):
            self.wt(t, (tag + "rows", tt))
            self.I(PE, t.matmul(self.pb[2][:], self.onesr[:], TMP,
                                start=True, stop=True))
            self.I(PE, t.matmul(self.pb[3][:], self.onesr[:], BB,
                                start=True, stop=True))
            self.mark((tag + "bc", tt), PE)

    def _p2_tt(self, tt, xns, nb, ntmp, n2t, sqb, rws, rlog, lgT, smt,
               combsub):
        blk = self.block
        LD, SX, SC, PE, VE, SE = (self.LD, self.SX, self.SC, self.PE,
                                  self.VE, self.SE)
        pb = self.pb
        MU = rws[:, 0:TTW]; TMP = rws[:, TTW:2 * TTW]

        @blk.sync
        def _(sy):
            if SC.n:
                sy.wait_ge(SC.h, SC.n)
            if tt >= 1:
                self.wt(sy, ("p2n2", tt - 1, NDC - 1))
            for dc in range(NDC):
                self.D(SX, sy.dma_start(
                    out=xns[:, dc * TTW:(dc + 1) * TTW],
                    in_=self.scr["xn"].ap()[dc, :, tt * TTW:(tt + 1) * TTW]))
            self.mark(("p2xn", tt), SX)

        @blk.tensor
        def _(t):
            if tt == 0:
                self.wt(t, "p2mixw")
            if ("p2n2done", tt - 1) in self.M:
                self.wt(t, ("p2n2done", tt - 1))
            for dc in range(NDC):
                mm = t.matmul(pb[0][0:1, :], self.onesb[:],
                              self.xnb[:, dc * S + tt * TTW:
                                       dc * S + (tt + 1) * TTW],
                              start=(dc == 0), stop=(dc == NDC - 1))
            self.I(PE, mm)
        for dc in range(NDC):
            @blk.scalar
            def _(s):
                if dc >= 2:
                    self.wt(s, ("p2sq1mm", tt, dc - 2))
                self.OP(s, SE, lambda: s.activation(
                    sqb[dc % 2][:],
                    self.xnb[:, dc * S + tt * TTW: dc * S + (tt + 1) * TTW],
                    AF.Square))
                self.mark(("p2sq1", tt, dc), SE)

            @blk.tensor
            def _(t):
                self.wt(t, ("p2sq1", tt, dc))
                mm = t.matmul(pb[1][0:1, :], self.onesb[:], sqb[dc % 2][:],
                              start=(dc == 0), stop=(dc == NDC - 1))
                self.I(PE, mm)
                self.mark(("p2sq1mm", tt, dc), PE)
        self.mark(("p2ln1st", tt), PE)

        self._p2_rows("p2ln1", tt, rws, pb[0][0:1, :], pb[1][0:1, :])

        @blk.vector
        def _(v):
            self.wt(v, ("p2ln1bc", tt))
            self.wt(v, ("p2xn", tt))
            for dc in range(NDC):
                xd = xns[:, dc * TTW:(dc + 1) * TTW]
                self.OP(v, VE, lambda: v.tensor_tensor(ntmp[:], xd, pb[2][:],
                                                       ALU.mult))
                self.OP(v, VE, lambda: v.tensor_tensor(ntmp[:], ntmp[:],
                                                       pb[3][:], ALU.add))
                self.OP(v, VE, lambda: v.tensor_scalar(
                    ntmp[:], ntmp[:], self.pcol("g1", dc),
                    self.pcol("be1", dc), ALU.mult, ALU.add))
                self.OP(v, VE, lambda: v.tensor_copy(
                    nb[:, dc * TTW:(dc + 1) * TTW], ntmp[:]))
                self.OP(v, VE, lambda: v.tensor_tensor(xd, xd, ntmp[:],
                                                       ALU.add))
            self.mark(("p2nb", tt), VE)

        for oc in range(NDC):
            @blk.tensor
            def _(t):
                if oc == 0:
                    self.wt(t, ("p2nb", tt))
                if oc >= 2:
                    self.wt(t, ("p2x2e", tt, oc - 2))
                for dc in range(NDC):
                    mm = t.matmul(pb[4 + oc % 2][:],
                                  self.wbuf[0][:, dc * DM + oc * 128:
                                               dc * DM + (oc + 1) * 128],
                                  nb[:, dc * TTW:(dc + 1) * TTW],
                                  start=(dc == 0), stop=(dc == NDC - 1))
                self.I(PE, mm)
                self.mark(("p2mix", tt, oc), PE)

            @blk.scalar
            def _(s):
                self.wt(s, ("p2mix", tt, oc))
                if oc >= 1:
                    self.wt(s, ("p2x2", tt, oc - 1))
                self.OP(s, SE, lambda: s.activation(
                    ntmp[:], pb[4 + oc % 2][:], AF.Identity,
                    bias=self.pcol("mixb", oc)))
                self.mark(("p2x2e", tt, oc), SE)

            @blk.vector
            def _(v):
                self.wt(v, ("p2x2e", tt, oc))
                if oc >= 1:
                    self.wt(v, ("p2sq2", tt, oc - 1))
                    self.wt(v, ("p2st2a", tt, oc - 1))
                xd = xns[:, oc * TTW:(oc + 1) * TTW]
                self.OP(v, VE, lambda: v.tensor_tensor(xd, xd, ntmp[:],
                                                       ALU.add))
                self.mark(("p2x2", tt, oc), VE)
                self.OP(v, VE, lambda: v.tensor_copy(sqb[0][:], xd))
                self.mark(("p2x2b", tt, oc), VE)

            @blk.tensor
            def _(t):
                self.wt(t, ("p2x2b", tt, oc))
                mm = t.matmul(pb[0][0:1, :], self.onesb[:], sqb[0][:],
                              start=(oc == 0), stop=(oc == NDC - 1))
                self.I(PE, mm)
                self.mark(("p2st2a", tt, oc), PE)

            @blk.scalar
            def _(s):
                self.wt(s, ("p2x2b", tt, oc))
                if oc >= 2:
                    self.wt(s, ("p2sq2mm", tt, oc - 1))
                self.OP(s, SE, lambda: s.activation(sqb[1][:], sqb[0][:],
                                                    AF.Square))
                self.mark(("p2sq2", tt, oc), SE)

            @blk.tensor
            def _(t):
                self.wt(t, ("p2sq2", tt, oc))
                self.wt(t, ("p2st2a", tt, oc))
                mm = t.matmul(pb[1][0:1, :], self.onesb[:], sqb[1][:],
                              start=(oc == 0), stop=(oc == NDC - 1))
                self.I(PE, mm)
                self.mark(("p2sq2mm", tt, oc), PE)

            @blk.sync
            def _(sy):
                self.wt(sy, ("p2x2", tt, oc))
                self.D(SC, sy.dma_start(
                    out=self.scr["x2"].ap()[oc, :, tt * TTW:(tt + 1) * TTW],
                    in_=xns[:, oc * TTW:(oc + 1) * TTW]))
        self.mark(("p2ln2st", tt), PE)

        self._p2_rows("p2ln2", tt, rws, pb[0][0:1, :], pb[1][0:1, :])

        for dc in range(NDC):
            @blk.vector
            def _(v):
                if dc == 0:
                    self.wt(v, ("p2ln2bc", tt))
                else:
                    self.wt(v, ("p2rmm", tt, dc - 1))
                xd = xns[:, dc * TTW:(dc + 1) * TTW]
                self.OP(v, VE, lambda: v.tensor_tensor(n2t[:], xd, pb[2][:],
                                                       ALU.mult))
                self.OP(v, VE, lambda: v.tensor_tensor(n2t[:], n2t[:],
                                                       pb[3][:], ALU.add))
                self.OP(v, VE, lambda: v.tensor_scalar(
                    n2t[:], n2t[:], self.pcol("g2", dc),
                    self.pcol("be2", dc), ALU.mult, ALU.add))
                self.OP(v, VE, lambda: v.tensor_copy(
                    self.n2b[:, dc * S + tt * TTW: dc * S + (tt + 1) * TTW],
                    n2t[:]))
                self.mark(("p2n2", tt, dc), VE)

            @blk.tensor
            def _(t):
                self.wt(t, ("p2n2", tt, dc))
                mm = t.matmul(pb[6][0:NE, :],
                              self.rw_sb[:, dc * NE:(dc + 1) * NE],
                              n2t[:], start=(dc == 0), stop=(dc == NDC - 1))
                self.I(PE, mm)
                self.mark(("p2rmm", tt, dc), PE)
        self.mark(("p2n2done", tt), VE)

        @blk.vector
        def _(v):
            self.wt(v, ("p2rmm", tt, NDC - 1))
            self.OP(v, VE, lambda: v.tensor_copy(rlog[:], pb[6][0:NE, :]))
            self.mark(("p2rlog", tt), VE)

        @blk.tensor
        def _(t):
            self.wt(t, ("p2rlog", tt))
            for c in range(4):
                self.I(PE, t.transpose(pb[7][:, c * 8:(c + 1) * 8],
                                       rlog[:, c * 128:(c + 1) * 128],
                                       self.eye[0:NE, 0:NE]))
            self.mark(("p2lgT", tt), PE)

        @blk.vector
        def _(v):
            self.wt(v, ("p2lgT", tt))
            self.OP(v, VE, lambda: v.tensor_copy(lgT[:, 16:48],
                                                 pb[7][:, 0:32]))
            for c in range(4):
                L = lgT[:, 16 + c * 8:24 + c * 8]
                self.OP(v, VE, lambda: v.reduce_max(lgT[:, c:c + 1], L,
                                                    axis=AXX))
                self.OP(v, VE, lambda: v.tensor_scalar(
                    lgT[:, 4 + c:5 + c], lgT[:, c:c + 1], -1.0, None,
                    ALU.mult))
            self.mark(("p2negm", tt), VE)

        @blk.scalar
        def _(s):
            self.wt(s, ("p2negm", tt))
            for c in range(4):
                self.OP(s, SE, lambda: s.activation(
                    smt[:, c * 8:(c + 1) * 8],
                    lgT[:, 16 + c * 8:24 + c * 8], AF.Exp,
                    bias=lgT[:, 4 + c:5 + c]))
            self.mark(("p2exp", tt), SE)

        @blk.vector
        def _(v):
            self.wt(v, ("p2exp", tt))
            for c in range(4):
                e = smt[:, c * 8:(c + 1) * 8]
                ssum = lgT[:, 8:9]; rs = lgT[:, 9:10]
                m1 = lgT[:, 10:11]; m2 = lgT[:, 11:12]
                probs = lgT[:, 16 + c * 8:24 + c * 8]
                msk = smt[:, 32:40]
                self.OP(v, VE, lambda: v.reduce_sum(ssum, e, axis=AXX))
                self.OP(v, VE, lambda: v.reciprocal(rs, ssum))
                self.OP(v, VE, lambda: v.tensor_scalar(probs, e, rs, None,
                                                       ALU.mult))
                self.OP(v, VE, lambda: v.reduce_max(m1, probs, axis=AXX))
                self.OP(v, VE, lambda: v.tensor_scalar(msk, probs, m1, None,
                                                       ALU.is_ge))
                self.OP(v, VE, lambda: v.scalar_tensor_tensor(
                    msk, msk, -2.0, probs, ALU.mult, ALU.add))
                self.OP(v, VE, lambda: v.reduce_max(m2, msk, axis=AXX))
                self.OP(v, VE, lambda: v.tensor_scalar(msk, probs, m2, None,
                                                       ALU.is_ge))
                self.OP(v, VE, lambda: v.tensor_tensor(
                    combsub[:, c * 8:(c + 1) * 8], probs, msk, ALU.mult))
            self.mark(("p2comb", tt), VE)

        @blk.tensor
        def _(t):
            self.wt(t, ("p2comb", tt))
            for c in range(4):
                self.I(PE, t.transpose(pb[5][0:8, c * 128:(c + 1) * 128],
                                       combsub[:, c * 8:(c + 1) * 8],
                                       self.eye[:]))
            self.mark(("p2combT", tt), PE)

        @blk.vector
        def _(v):
            self.wt(v, ("p2combT", tt))
            self.OP(v, VE, lambda: v.tensor_copy(
                self.comb[:, tt * TTW:(tt + 1) * TTW], pb[5][0:8, :]))
            self.mark(("p2combout", tt), VE)

    def p3_attn(self):
        blk = self.block
        i = self.inp
        LD, SX, SC, PE, VE, SE = (self.LD, self.SX, self.SC, self.PE,
                                  self.VE, self.SE)
        st = self.stP3 = ExitStack()
        self.dexq = self.sb("dexq_s", [128, UW], BF16, st)
        self.dexk = self.sb("dexk_s", [128, UW], BF16, st)
        qh = [self.sb(f"qh{k}", [128, S], BF16, st) for k in range(2)]
        qxs = [self.sb(f"qxs{k}", [128, KW], BF16, st) for k in range(2)]
        kxs = [self.sb(f"kxs{k}", [128, KW], BF16, st) for k in range(2)]
        rstrip = ([self.sb(f"rqs{k}", [128, S], BF16, st) for k in range(3)],
                  [self.sb(f"rks{k}", [128, S], BF16, st) for k in range(3)])
        s2f = self.sb("s2fb", [128, 8 * TTW], BF16, st)
        prb = self.sb("prbb", [128, 8 * TTW], BF16, st)
        denr = self.sb("denr", [1, TTW], F32, st)
        denb = self.sb("denb", [DH, TTW], BF16, st)
        ctxo = self.sb("ctxo", [DH, TTW], BF16, st)
        c8 = self.sb("c8", [1, 2], F32, st)
        cb = self.sb("cb", [128, 1], F32, st)

        @blk.gpsimd
        def _(g):
            for half in range(2):
                b0 = half * 64
                self.D(LD, g.dma_start(out=self.dexq[b0:b0 + 64, :],
                                       in_=i["dexq"].ap()))
                self.D(LD, g.dma_start(out=self.dexk[b0:b0 + 64, :],
                                       in_=i["dexk"].ap()))
        self.mark("p3dex", LD)

        @blk.vector
        def _(v):
            v.engine_nop().then_inc(self.VE.h, 1)
            self.VE.n += 1
            self.mark("p3ones", VE)

        self.megaq = []
        self.megax = []
        self.p3qxs = qxs
        for h in range(NH):
            self._p3_gen(h, qh, qxs, kxs)
            self._p3_scores(h, qh, rstrip, s2f, prb, denr, denb, ctxo,
                            c8, cb)
        self.barrier()

    def _p3_gen(self, h, qh, qxs, kxs):
        blk = self.block
        SX, SC, PE, VE, SE = self.SX, self.SC, self.PE, self.VE, self.SE
        pb = self.pb
        b0 = (h % 2) * 64
        kslice = lambda c0, c1: self.K_sb[
            b0:b0 + 64, (h // 2) * S + c0:(h // 2) * S + c1]
        WIN = 2175
        pw = [512, 512, 512, 512, 127]

        @blk.sync
        def _(sy):
            if h >= 2:
                self.wt(sy, ("p3qdone", h - 2))
            self.D(SX, sy.dma_start(out=qh[h % 2][b0:b0 + 64, :],
                                    in_=self.scr["qs"].ap()[
                                        h * DH:(h + 1) * DH, :]))
            self.mark(("p3qh", h), SX)

        pgc = [0]
        for which, dstd, srcq in (("q", self.scr["rq"], True),
                                  ("k", self.scr["rk"], False)):
            stg = qxs if srcq else kxs
            for C in range(NTC):
                u0 = 1920 - 128 * C
                w = h * NTC + C
                pgs = []
                @blk.tensor
                def _(t):
                    for f in range(5):
                        pg = pgc[0]; pgc[0] += 1
                        pgs.append(pg)
                        if h == 0 and pg == 0:
                            self.wt(t, "p3dex")
                        if pg == 0:
                            if srcq:
                                self.wt(t, ("p3qh", h))
                            if h >= 1:
                                self.wt(t, ("p3ctxo", h - 1, NLG - 1))
                                self.wt(t, ("p3dene", h - 1, NLG - 1))
                        if pg >= 6:
                            if pg - 6 < 80:
                                t.wait_ge(self.SEX.h, h * 96 + pg - 5)
                            else:
                                self.wt(t, ("p3ge", h, pg - 6))
                        lhsT = (qh[h % 2][b0:b0 + 64,
                                          C * 128:(C + 1) * 128]
                                if srcq else kslice(C * 128, (C + 1) * 128))
                        tbl = self.dexq if srcq else self.dexk
                        j0 = f * 512
                        mm = t.matmul(
                            pb[2 + pg % 6][:, 0:pw[f]], lhsT,
                            tbl[b0:b0 + 64, u0 + j0: u0 + j0 + pw[f]],
                            start=True, stop=True)
                        self.I(PE, mm)
                        self.mark(("p3gm", h, pg), PE)

                if srcq:
                    for k, pg in enumerate(pgs):
                        self.megaq.append((w, k, pg))
                elif False:
                    pass
                else:
                    @blk.vector
                    def _(v):
                        if w >= 2:
                            self.wt(v, ("p3gw", which, w - 2))
                        for k, pg in enumerate(pgs):
                            self.wt(v, ("p3gm", h, pg))
                            self.OP(v, VE, lambda: v.tensor_copy(
                                stg[w % 2][:, k * 512: k * 512 + pw[k]],
                                pb[2 + pg % 6][:, 0:pw[k]]))
                            self.mark(("p3ge", h, pg), VE)

                @blk.sync
                def _(sy):
                    if srcq:
                        sy.wait_ge(self.SEX.h, h * 96 + pgs[-1] + 1)
                    else:
                        self.wt(sy, ("p3ge", h, pgs[-1]))
                    dst = AP(dstd, h * S * 2304 + C * 128 * 2304 + 1,
                             [[2305, 128], [1, WIN]])
                    self.D(SC, sy.dma_start(out=dst,
                                            in_=stg[w % 2][:, 0:WIN]))
                    self.mark(("p3gw", which, w), SC)
            self.mark(("p3genq" if srcq else "p3genk", h), SC)

    def _p3_scores(self, h, qh, rstrip, s2f, prb, denr, denb, ctxo, c8, cb):
        blk = self.block
        SX, SC, PE, VE, SE = self.SX, self.SC, self.PE, self.VE, self.SE
        pb = self.pb
        b0 = (h % 2) * 64
        kslice = lambda c0, c1: self.K_sb[
            b0:b0 + 64, (h // 2) * S + c0:(h // 2) * S + c1]
        NR = 2

        for rc in range(NTC):
            R0 = rc * 128
            @blk.sync
            def _(sq):
                if rc >= NR:
                    self.wt(sq, ("p3s2", (h * NTC + rc - NR) * NLG + 3))
                self.wt(sq, ("p3genk", h))
                self.D(self.SD, sq.dma_start(
                    out=rstrip[1][rc % NR][:],
                    in_=AP(self.scr["rk"], h * S * 2304 + R0 * 2304 + 128,
                           [[2304, 128], [1, S]])))
                self.mark(("p3rd", h * NTC + rc), self.SD)
            @blk.sync
            def _(sy):
                if rc >= NR:
                    self.wt(sy, ("p3s2", (h * NTC + rc - NR) * NLG + 3))
                self.wt(sy, ("p3genq", h))
                for piece in range(4):
                    self.D(SX, sy.dma_start(
                        out=rstrip[0][rc % NR][:, piece * 512:
                                               (piece + 1) * 512],
                        in_=AP(self.scr["rq"],
                               h * S * 2304 + (piece * 512) * 2304
                               + 128 + R0,
                               [[2304, 512], [1, 128]]),
                        transpose=True))
                self.mark(("p3rdq", h * NTC + rc), SX)


            rq, rk = rstrip
            for lg in range(NLG):
                L0 = lg * TTW
                g = (h * NTC + rc) * NLG + lg

                @blk.tensor
                def _(t):
                    if rc == 0:
                        self.wt(t, ("p3genq", h))
                        self.wt(t, ("p3genk", h))
                    if g >= 2:
                        self.wt(t, ("p3s1", g - 2))
                    mm = t.matmul(pb[g % 2][:],
                                  kslice(R0, R0 + 128),
                                  qh[h % 2][b0:b0 + 64, L0:L0 + TTW],
                                  start=True, stop=True)
                    self.I(PE, mm)
                    self.mark(("p3qk", g), PE)

                @blk.vector
                def _(v):
                    self.wt(v, ("p3qk", g))
                    if lg == 0:
                        self.wt(v, ("p3rd", h * NTC + rc))
                        self.wt(v, ("p3rdq", h * NTC + rc))
                        if rc >= 2:
                            v.wait_ge(self.SEX.h, h * 96 + 80 + rc - 1)
                        elif h >= 1:
                            v.wait_ge(self.SEX.h,
                                      (h - 1) * 96 + 80 + rc + 15)
                    self.OP(v, VE, lambda: v.tensor_tensor(
                        s2f[:, (g % 8) * TTW:(g % 8 + 1) * TTW],
                        pb[g % 2][:], rk[rc % NR][:, L0:L0 + TTW], ALU.add))
                    self.mark(("p3s1", g), VE)

            half = (rc % 2) * 4 * TTW
            @blk.vector
            def _(v):
                self.OP(v, VE, lambda: v.tensor_tensor(
                    s2f[:, half:half + 4 * TTW],
                    s2f[:, half:half + 4 * TTW], rq[rc % NR][:], ALU.add))
                for lg in range(NLG):
                    self.mark(("p3s2", (h * NTC + rc) * NLG + lg), VE)

            if rc == 0:
                g0 = (h * NTC) * NLG
                @blk.tensor
                def _(t):
                    self.wt(t, ("p3s2", g0))
                    self.wt(t, "p3ones")
                    mm = t.matmul(pb[6][0:1, :], self.onesb[:],
                                  s2f[:, 0:TTW], start=True, stop=True)
                    self.I(PE, mm)
                    self.mark(("p3cs", h), PE)

                @blk.vector
                def _(v):
                    self.wt(v, ("p3cs", h))
                    self.OP(v, VE, lambda: v.reduce_sum(
                        c8[:, 0:1], pb[6][0:1, :], axis=AXX))
                    self.OP(v, VE, lambda: v.tensor_scalar(
                        c8[:, 1:2], c8[:, 0:1],
                        -1.0 / (128.0 * TTW * 8.0), None, ALU.mult))
                    self.mark(("p3c8", h), VE)

                @blk.tensor
                def _(t):
                    self.wt(t, ("p3c8", h))
                    mm = t.matmul(pb[6][:, 0:1], self.onesr[:],
                                  c8[:, 1:2], start=True, stop=True)
                    self.I(PE, mm)
                    self.mark(("p3cbm", h), PE)

                @blk.vector
                def _(v):
                    self.wt(v, ("p3cbm", h))
                    self.OP(v, VE, lambda: v.tensor_copy(
                        cb[:], pb[6][:, 0:1]))
                    self.mark(("p3cb", h), VE)

            self.megax.append((h, rc, half))

            @blk.tensor
            def _(t):
                t.wait_ge(self.SEX.h, h * 96 + 80 + rc + 1)
                for lg in range(NLG):
                    g = (h * NTC + rc) * NLG + lg
                    mm = t.matmul(pb[2 + lg][0:65, :],
                                  self.VT[:, (rc * NH + h) * 65:
                                          (rc * NH + h) * 65 + 65],
                                  prb[:, half + lg * TTW:
                                      half + (lg + 1) * TTW],
                                  start=(rc == 0), stop=(rc == NTC - 1))
                    self.I(PE, mm)
                    self.mark(("p3ctx", g), PE)

        SEX = self.SEX
        qxs2 = self.p3qxs
        pbL = self.pb
        @blk.scalar
        def _(s):
            pwl = [512, 512, 512, 512, 127]
            for (w2, k2, pg2) in self.megaq:
                if w2 >= 2:
                    self.wt(s, ("p3gw", "q", w2 - 2))
                self.wt(s, ("p3gm", h, pg2))
                if SEX.n:
                    s.wait_ge(SEX.h, SEX.n)
                ins = s.activation(
                    qxs2[w2 % 2][:, k2 * 512: k2 * 512 + pwl[k2]],
                    pbL[2 + pg2 % 6][:, 0:pwl[k2]], AF.Copy)
                SEX.n += 1
                ins.then_inc(SEX.h, 1)
            self.megaq = []
            self.wt(s, ("p3cb", h))
            for (hh2, rc2, half2) in self.megax:
                self.wt(s, ("p3s2", (hh2 * NTC + rc2) * NLG + 3))
                if ("p3ctx", (hh2 * NTC + rc2 - 2) * NLG + 3) in self.M:
                    self.wt(s, ("p3ctx", (hh2 * NTC + rc2 - 2) * NLG + 3))
                if SEX.n:
                    s.wait_ge(SEX.h, SEX.n)
                ins = s.activation(prb[:, half2:half2 + 4 * TTW],
                                   s2f[:, half2:half2 + 4 * TTW], AF.Exp,
                                   bias=cb[:], scale=0.125)
                SEX.n += 1
                ins.then_inc(SEX.h, 1)
            self.megax = []
        assert SEX.n == (h + 1) * 96, (h, SEX.n)

        for lg in range(NLG):
            @blk.vector
            def _(v):
                self.wt(v, ("p3ctx", (h * NTC + NTC - 1) * NLG + lg))
                self.OP(v, VE, lambda: v.reciprocal(denr[:],
                                                    pb[2 + lg][64:65, :]))
                self.mark(("p3denr", h, lg), VE)

            @blk.tensor
            def _(t):
                self.wt(t, ("p3denr", h, lg))
                mm = t.matmul(pb[6][0:DH, :], self.onesr[:, 0:DH], denr[:],
                              start=True, stop=True)
                self.I(PE, mm)
                self.mark(("p3denb", h, lg), PE)

            @blk.vector
            def _(v):
                self.wt(v, ("p3denb", h, lg))
                self.OP(v, VE, lambda: v.tensor_copy(denb[:],
                                                     pb[6][0:DH, :]))
                self.mark(("p3dene", h, lg), VE)
                if (h, lg) != (0, 0):
                    prev = (h, lg - 1) if lg >= 1 else (h - 1, NLG - 1)
                    self.wt(v, ("p3cxw",) + prev)
                self.OP(v, VE, lambda: v.tensor_tensor(
                    ctxo[:], pb[2 + lg][0:DH, :], denb[:], ALU.mult))
                self.mark(("p3ctxo", h, lg), VE)

            @blk.sync
            def _(sy):
                self.wt(sy, ("p3ctxo", h, lg))
                r0 = (h % 2) * 64
                self.D(SC, sy.dma_start(
                    out=self.scr["cx"].ap()[h // 2, r0:r0 + 64,
                                            lg * TTW:(lg + 1) * TTW],
                    in_=ctxo[:]))
                self.mark(("p3cxw", h, lg), SC)
        self.mark(("p3qdone", h), VE)

    def p3_oproj(self):
        blk = self.block
        i = self.inp
        LD, SX, SC, PE, VE, SE = (self.LD, self.SX, self.SC, self.PE,
                                  self.VE, self.SE)
        st = ExitStack()
        cxt = self.sb("cxt", [128, NDC * TTW], BF16, st)
        xnt2 = [self.sb(f"xnt2{k}", [128, TTW], F32, st) for k in range(2)]
        hset = self.sb("hset", [128, NDC * TTW], F32, st)
        hb = [self.sb(f"hb{k}", [128, TTW], BF16, st) for k in range(2)]
        rws = self.sb("rws3", [1, 3 * TTW], F32, st)
        sstg = [self.sb(f"sstg{k}", [128, TTW], F32, st) for k in range(2)]
        wob = self.sb("wob", [128, NDC * DM], BF16, st)
        MU = rws[:, 0:TTW]; TMP = rws[:, TTW:2 * TTW]
        BB = rws[:, 2 * TTW:3 * TTW]
        pb = self.pb

        @blk.gpsimd
        def _(g):
            for dc in range(NDC):
                self.D(LD, g.dma_start(
                    out=wob[:, dc * DM:(dc + 1) * DM],
                    in_=i["Wo"].ap()[dc * 128:(dc + 1) * 128, :]))
        self.mark("p3wo", LD)

        for tt in range(NTT):
            @blk.sync
            def _(sy):
                if tt >= 1:
                    self.wt(sy, ("p3om", tt - 1, NDC - 1))
                for dc in range(NDC):
                    self.D(SX, sy.dma_start(
                        out=cxt[:, dc * TTW:(dc + 1) * TTW],
                        in_=self.scr["cx"].ap()[dc, :,
                                                tt * TTW:(tt + 1) * TTW]))
                self.mark(("p3cxr", tt), SX)

            for half in range(2):
                @blk.tensor
                def _(t):
                    if tt == 0 and half == 0:
                        self.wt(t, "p3wo")
                    self.wt(t, ("p3cxr", tt))
                    if (tt, half) != (0, 0):
                        ph = (tt, half * 4 - 1) if half == 1 else                              (tt - 1, NDC - 1)
                        self.wt(t, ("p3he",) + ph)
                    for dc in range(NDC):
                        for o4 in range(4):
                            oc = half * 4 + o4
                            mm = t.matmul(
                                pb[o4][:],
                                wob[:, dc * DM + oc * 128:
                                    dc * DM + (oc + 1) * 128],
                                cxt[:, dc * TTW:(dc + 1) * TTW],
                                start=(dc == 0), stop=(dc == NDC - 1))
                            if dc == NDC - 1:
                                self.I(PE, mm)
                                self.mark(("p3om", tt, oc), PE)

                for o4 in range(4):
                    oc = half * 4 + o4
                    gidx = tt * NDC + oc

                    @blk.sync
                    def _(sy):
                        if gidx >= 2:
                            self.wt(sy, ("p3hx", gidx - 2))
                        self.D(SX, sy.dma_start(
                            out=xnt2[oc % 2][:],
                            in_=self.scr["xn"].ap()[oc, :,
                                                    tt * TTW:(tt + 1) * TTW]))
                        self.mark(("p3xnr", gidx), SX)

                    @blk.scalar
                    def _(s):
                        self.wt(s, ("p3om", tt, oc))
                        self.OP(s, SE, lambda: s.activation(
                            hset[:, oc * TTW:(oc + 1) * TTW], pb[o4][:],
                            AF.Identity, bias=self.pcol("bo", oc)))
                        self.mark(("p3he", tt, oc), SE)

                    @blk.vector
                    def _(v):
                        self.wt(v, ("p3he", tt, oc))
                        self.wt(v, ("p3xnr", gidx))
                        hd = hset[:, oc * TTW:(oc + 1) * TTW]
                        self.OP(v, VE, lambda: v.tensor_tensor(
                            hd, hd, xnt2[oc % 2][:], ALU.add))
                        self.mark(("p3hx", gidx), VE)
                        if oc >= 1:
                            self.wt(v, ("p3hst", tt, oc - 1))
                            self.wt(v, ("p3hsq", tt, oc - 1))
                        self.OP(v, VE, lambda: v.tensor_copy(hb[0][:], hd))
                        self.mark(("p3hb", tt, oc), VE)

                    @blk.tensor
                    def _(t):
                        self.wt(t, ("p3hb", tt, oc))
                        if tt >= 1 and oc == 0:
                            self.wt(t, ("p3rows", tt - 1))
                        mm = t.matmul(pb[4][0:1, :], self.onesb[:],
                                      hb[0][:], start=(oc == 0),
                                      stop=(oc == NDC - 1))
                        self.I(PE, mm)
                        self.mark(("p3hst", tt, oc), PE)

                    @blk.scalar
                    def _(s):
                        self.wt(s, ("p3hb", tt, oc))
                        if oc >= 1:
                            self.wt(s, ("p3hsq2", tt, oc - 1))
                        self.OP(s, SE, lambda: s.activation(hb[1][:],
                                                            hb[0][:],
                                                            AF.Square))
                        self.mark(("p3hsq", tt, oc), SE)

                    @blk.tensor
                    def _(t):
                        self.wt(t, ("p3hsq", tt, oc))
                        mm = t.matmul(pb[5][0:1, :], self.onesb[:],
                                      hb[1][:], start=(oc == 0),
                                      stop=(oc == NDC - 1))
                        self.I(PE, mm)
                        self.mark(("p3hsq2", tt, oc), PE)
            self.mark(("p3lnst", tt), PE)

            @blk.vector
            def _(v):
                self.wt(v, ("p3lnst", tt))
                self.OP(v, VE, lambda: v.tensor_scalar(
                    MU, pb[4][0:1, :], 1.0 / DM, None, ALU.mult))
                self.OP(v, VE, lambda: v.tensor_scalar(
                    TMP, pb[5][0:1, :], 1.0 / DM, None, ALU.mult))
                self.OP(v, VE, lambda: v.tensor_tensor(BB, MU, MU, ALU.mult))
                self.OP(v, VE, lambda: v.scalar_tensor_tensor(
                    TMP, BB, -1.0, TMP, ALU.mult, ALU.add))
                self.mark(("p3var", tt), VE)

            @blk.scalar
            def _(s):
                self.wt(s, ("p3var", tt))
                self.OP(s, SE, lambda: s.activation(
                    TMP, TMP, AF.Sqrt, bias=self.cst[0:1, 1:2]))
                self.OP(s, SE, lambda: s.activation(
                    self.cstj[0:1, 0:1], self.cst[0:1, 0:1], AF.Copy))
                self.mark(("p3sd", tt), SE)

            @blk.vector
            def _(v):
                self.wt(v, ("p3sd", tt))
                self.OP(v, VE, lambda: v.reciprocal(TMP, TMP))
                self.OP(v, VE, lambda: v.scalar_tensor_tensor(
                    BB, MU, -1.0, TMP, ALU.mult, ALU.mult))
                self.mark(("p3rows", tt), VE)

            @blk.tensor
            def _(t):
                self.wt(t, ("p3rows", tt))
                if tt >= 1:
                    self.wt(t, ("p3s", tt - 1, NDC - 1))
                self.I(PE, t.matmul(pb[6][:], self.onesr[:], TMP,
                                    start=True, stop=True))
                self.I(PE, t.matmul(pb[7][:], self.onesr[:], BB,
                                    start=True, stop=True))
                self.mark(("p3bc", tt), PE)

            for oc in range(NDC):
                gidx = tt * NDC + oc

                @blk.vector
                def _(v):
                    if oc == 0:
                        self.wt(v, ("p3bc", tt))
                    hd = hset[:, oc * TTW:(oc + 1) * TTW]
                    k = oc % 2
                    if gidx >= 2:
                        self.wt(v, ("p3sw", gidx - 2))
                    self.OP(v, VE, lambda: v.tensor_tensor(
                        sstg[k][:], hd, pb[6][:], ALU.mult))
                    self.OP(v, VE, lambda: v.tensor_tensor(
                        sstg[k][:], sstg[k][:], pb[7][:], ALU.add))
                    self.OP(v, VE, lambda: v.tensor_scalar(
                        sstg[k][:], sstg[k][:], self.pcol("lng", oc),
                        self.pcol("lnb", oc), ALU.mult, ALU.add))
                    self.mark(("p3s", tt, oc), VE)

                @blk.sync
                def _(sy):
                    self.wt(sy, ("p3s", tt, oc))
                    self.D(SC, sy.dma_start(
                        out=self.scr["sa"].ap()[oc, :,
                                                tt * TTW:(tt + 1) * TTW],
                        in_=sstg[oc % 2][:]))
                    self.mark(("p3sw", gidx), SC)

        self.barrier()
        st.close()

    def p4_moe(self):
        blk = self.block
        i = self.inp
        LD, SX, SC, PE, VE, SE = (self.LD, self.SX, self.SC, self.PE,
                                  self.VE, self.SE)
        self.stP4 = ExitStack()
        self.oacc = self.sb("oacc", [128, NDC * S], F32, self.stP4)
        st = self.stP4b = ExitStack()
        hh = self.sb("hh", [128, 4 * S], BF16, st)
        w1b = [self.sb(f"w1b{k}", [128, 8 * TTW], BF16, st) for k in range(2)]
        w2b = [self.sb(f"w2b{k}", [128, 4 * DM], BF16, st) for k in range(2)]
        cbc = self.sb("cbc", [128, S], F32, st)
        htmp = [self.sb(f"htmp{k}", [128, TTW], F32, st) for k in range(4)]
        b2s = self.sb("b2s", [NE, DM], F32, st)
        sel8 = self.sb("sel8_s", [NE, NE * 128], F32, st)
        pb = self.pb
        LD0 = LD.n

        @blk.gpsimd
        def _(g):
            self.D(LD, g.dma_start(out=b2s[:], in_=i["b2"].ap()))
            self.D(LD, g.dma_start(out=sel8[:], in_=self.sel8_h.ap()))

        def stream_block(blkid):
            e, fw = divmod(blkid, NDC)
            @blk.gpsimd
            def _(g):
                if blkid >= 2:
                    self.wt(g, ("p4m1l", blkid - 2))
                    self.wt(g, ("p4m2l", blkid - 2))
                for dc in range(NDC):
                    self.D(LD, g.dma_start(
                        out=w1b[blkid % 2][:, dc * TTW:(dc + 1) * TTW],
                        in_=i["w1"].ap()[e, dc * 128:(dc + 1) * 128,
                                         fw * TTW:(fw + 1) * TTW]))
                for fsub in range(4):
                    self.D(LD, g.dma_start(
                        out=w2b[blkid % 2][:, fsub * DM:(fsub + 1) * DM],
                        in_=i["w2"].ap()[e, fw * TTW + fsub * 128:
                                         fw * TTW + (fsub + 1) * 128, :]))

        stream_block(0)
        stream_block(1)

        def ldgate(blkid):
            return LD0 + 32 + 16 * 12 * (blkid + 1)

        for dc in range(NDC):
            for tt in range(NTT):
                j = dc * NTT + tt
                @blk.tensor
                def _(t):
                    if dc == 0 and tt == 0:
                        t.wait_ge(LD.h, LD0 + 32)
                    if j >= 2:
                        self.wt(t, ("p4bi", j - 2))
                    mm = t.matmul(pb[j % 2][:],
                                  b2s[:, dc * 128:(dc + 1) * 128],
                                  self.comb[:, tt * TTW:(tt + 1) * TTW],
                                  start=True, stop=True)
                    self.I(PE, mm)
                    self.mark(("p4bm", j), PE)

                @blk.vector
                def _(v):
                    self.wt(v, ("p4bm", j))
                    self.OP(v, VE, lambda: v.tensor_copy(
                        self.oacc[:, dc * S + tt * TTW:
                                  dc * S + (tt + 1) * TTW], pb[j % 2][:]))
                    self.mark(("p4bi", j), VE)
        self.mark("p4binit", VE)

        for e in range(NE):
            for tt in range(NTT):
                @blk.tensor
                def _(t):
                    if e == 0 and tt == 0:
                        self.wt(t, "p4binit")
                    if e >= 1 and tt == 0:
                        self.wt(t, ("p4m2e", e * NDC - 1, NDC - 1))
                    if tt >= 2:
                        self.wt(t, ("p4cbe", e, tt - 2))
                    mm = t.matmul(pb[tt % 2][:],
                                  sel8[:, e * 128:(e + 1) * 128],
                                  self.comb[:, tt * TTW:(tt + 1) * TTW],
                                  start=True, stop=True)
                    self.I(PE, mm)
                    self.mark(("p4cbm", e, tt), PE)

                @blk.vector
                def _(v):
                    self.wt(v, ("p4cbm", e, tt))
                    self.OP(v, VE, lambda: v.tensor_copy(
                        cbc[:, tt * TTW:(tt + 1) * TTW], pb[tt % 2][:]))
                    self.mark(("p4cbe", e, tt), VE)

            for fw in range(NDC):
                blkid = e * NDC + fw
                for fsub in range(4):
                    bb = (fsub % 2) * 4
                    @blk.tensor
                    def _(t):
                        t.wait_ge(LD.h, ldgate(blkid))
                        if fsub == 0 and fw == 0:
                            self.wt(t, ("p4cbe", e, NTT - 1))
                        if fsub >= 2:
                            self.wt(t, ("p4hh", blkid, fsub - 2))
                        elif blkid >= 1:
                            self.wt(t, ("p4m2e", blkid - 1, NDC - 1))
                        for dc in range(NDC):
                            lhsT = w1b[blkid % 2][
                                :, dc * TTW + fsub * 128:
                                dc * TTW + (fsub + 1) * 128]
                            for tt in range(NTT):
                                mm = t.matmul(
                                    pb[bb + tt][:], lhsT,
                                    self.n2b[:, dc * S + tt * TTW:
                                             dc * S + (tt + 1) * TTW],
                                    start=(dc == 0), stop=(dc == NDC - 1))
                        self.I(PE, mm)
                        self.mark(("p4m1", blkid, fsub), PE)
                        if fsub == 3:
                            self.mark(("p4m1l", blkid), PE)

                    @blk.scalar
                    def _(s):
                        self.wt(s, ("p4m1", blkid, fsub))
                        prevk = ((blkid, fsub - 1) if fsub >= 1
                                 else (blkid - 1, 3))
                        if ("p4hh",) + prevk in self.M:
                            self.wt(s, ("p4hh",) + prevk)
                        for tt in range(NTT):
                            self.OP(s, SE, lambda: s.activation(
                                htmp[tt][:], pb[bb + tt][:], AF.Silu,
                                bias=self.b1perp[:, e * 32 + fw * 4 + fsub:
                                                 e * 32 + fw * 4 + fsub + 1]))
                            self.mark(("p4sl", blkid, fsub, tt), SE)

                    @blk.vector
                    def _(v):
                        for tt in range(NTT):
                            self.wt(v, ("p4sl", blkid, fsub, tt))
                            self.OP(v, VE, lambda: v.tensor_tensor(
                                hh[:, fsub * S + tt * TTW:
                                   fsub * S + (tt + 1) * TTW],
                                htmp[tt][:],
                                cbc[:, tt * TTW:(tt + 1) * TTW], ALU.mult))
                            self.mark(("p4hhm", blkid, fsub, tt), VE)
                    self.mark(("p4hh", blkid, fsub), VE)

                for dcout in range(NDC):
                    bb = (dcout % 2) * 4
                    @blk.tensor
                    def _(t):
                        if dcout == 0:
                            self.wt(t, ("p4hh", blkid, 3))
                        if dcout >= 2:
                            self.wt(t, ("p4m2e", blkid, dcout - 2))
                        for fsub in range(4):
                            lhsT = w2b[blkid % 2][
                                :, fsub * DM + dcout * 128:
                                fsub * DM + (dcout + 1) * 128]
                            for tt in range(NTT):
                                mm = t.matmul(
                                    pb[bb + tt][:], lhsT,
                                    hh[:, fsub * S + tt * TTW:
                                       fsub * S + (tt + 1) * TTW],
                                    start=(fsub == 0), stop=(fsub == 3))
                        self.I(PE, mm)
                        self.mark(("p4m2", blkid, dcout), PE)
                        if dcout == NDC - 1:
                            self.mark(("p4m2l", blkid), PE)

                    @blk.vector
                    def _(v):
                        self.wt(v, ("p4m2", blkid, dcout))
                        for tt in range(NTT):
                            a = self.oacc[:, dcout * S + tt * TTW:
                                          dcout * S + (tt + 1) * TTW]
                            self.OP(v, VE, lambda: v.tensor_tensor(
                                a, a, pb[bb + tt][:], ALU.add))
                        self.mark(("p4m2e", blkid, dcout), VE)

                if blkid + 2 < NE * NDC:
                    stream_block(blkid + 2)

        self.barrier()

    def p5_final(self):
        blk = self.block
        i = self.inp
        LD, SX, SC, PE, VE, SE, ST = (self.LD, self.SX, self.SC, self.PE,
                                      self.VE, self.SE, self.ST)
        st = ExitStack()
        x2t = self.sb("x2t", [128, NDC * TTW], F32, st)
        prod = self.sb("prod", [128, NDC * TTW], F32, st)
        gst = [self.sb(f"gst{k}", [128, TTW], F32, st) for k in range(2)]
        sst = [self.sb(f"sst{k}", [128, TTW], F32, st) for k in range(2)]
        xtk = [self.sb(f"xtk{k}", [128, DM], F32, st) for k in range(2)]
        ostg = [self.sb(f"ostg{k}", [128, DM], F32, st) for k in range(2)]
        pb = self.pb

        for tt in range(NTT):
            @blk.sync
            def _(sy):
                if tt >= 1:
                    self.wt(sy, ("p5ff", tt - 1, NDC - 1))
                for dc in range(NDC):
                    self.D(SX, sy.dma_start(
                        out=x2t[:, dc * TTW:(dc + 1) * TTW],
                        in_=self.scr["x2"].ap()[dc, :,
                                                tt * TTW:(tt + 1) * TTW]))
                self.mark(("p5x2", tt), SX)

            for dc in range(NDC):
                j = tt * NDC + dc

                @blk.sync
                def _(sy):
                    if j >= 2:
                        self.wt(sy, ("p5pr", j - 2))
                    self.D(SC, sy.dma_start(
                        out=sst[dc % 2][:],
                        in_=self.scr["sa"].ap()[dc, :,
                                                tt * TTW:(tt + 1) * TTW]))
                    self.mark(("p5sr", j), SX if False else SC)

                @blk.vector
                def _(v):
                    if dc == 0:
                        self.wt(v, ("p5x2", tt))
                    xd = x2t[:, dc * TTW:(dc + 1) * TTW]
                    if j >= 2:
                        self.wt(v, ("p5g", j - 2))
                    self.OP(v, VE, lambda: v.tensor_tensor(
                        gst[dc % 2][:], xd,
                        self.oacc[:, dc * S + tt * TTW:
                                  dc * S + (tt + 1) * TTW], ALU.add))
                    self.mark(("p5ff", tt, dc), VE)

                @blk.scalar
                def _(s):
                    self.wt(s, ("p5ff", tt, dc))
                    self.OP(s, SE, lambda: s.activation(
                        gst[dc % 2][:], gst[dc % 2][:], AF.Silu))
                    self.mark(("p5g", j), SE)

                @blk.vector
                def _(v):
                    self.wt(v, ("p5g", j))
                    self.wt(v, ("p5sr", j))
                    if tt >= 1:
                        self.wt(v, ("p5tp", tt - 1, 3))
                    self.OP(v, VE, lambda: v.tensor_tensor(
                        prod[:, dc * TTW:(dc + 1) * TTW], gst[dc % 2][:],
                        sst[dc % 2][:], ALU.mult))
                    self.mark(("p5pr", j), VE)

            for tsub in range(4):
                tc = tt * 4 + tsub
                @blk.sync
                def _(sy):
                    if tc >= 2:
                        self.wt(sy, ("p5os", tc - 2))
                    self.D(SX, sy.dma_start(
                        out=xtk[tsub % 2][:],
                        in_=i["x"].ap()[tc * 128:(tc + 1) * 128, :]))
                    self.mark(("p5xr", tc), SX)

                @blk.tensor
                def _(t):
                    self.wt(t, ("p5pr", tt * NDC + NDC - 1))
                    if tc >= 2:
                        self.wt(t, ("p5oe", tc - 2))
                    bb = (tsub % 2) * 2
                    for dc in range(NDC):
                        mm = t.transpose(
                            pb[bb + dc // 4][:, (dc % 4) * 128:
                                             (dc % 4 + 1) * 128],
                            prod[:, dc * TTW + tsub * 128:
                                 dc * TTW + (tsub + 1) * 128],
                            self.eye[:])
                        if dc % 4 == 3:
                            self.I(PE, mm)
                    self.mark(("p5tp", tt, tsub), PE)

                @blk.vector
                def _(v):
                    self.wt(v, ("p5tp", tt, tsub))
                    self.wt(v, ("p5xr", tc))
                    bb = (tsub % 2) * 2
                    for half in range(2):
                        self.OP(v, VE, lambda: v.tensor_tensor(
                            ostg[tsub % 2][:, half * 512:(half + 1) * 512],
                            pb[bb + half][:],
                            xtk[tsub % 2][:, half * 512:(half + 1) * 512],
                            ALU.add))
                    self.mark(("p5oe", tc), VE)

                @blk.sync
                def _(sy):
                    self.wt(sy, ("p5oe", tc))
                    self.D(ST, sy.dma_start(
                        out=self.out_h.ap()[tc * 128:(tc + 1) * 128, :],
                        in_=ostg[tsub % 2][:]))
                    self.mark(("p5os", tc), ST)

        @blk.sync
        def _(sy):
            self.wait_all(sy)
            sy.wait_ge(ST.h, ST.n)
        st.close()

    def finish_dev(self):
        blk = self.block
        ST = self.ST
        dumps = []
        for name in DEV_TAPS:
            t = getattr(self, {"xnb": "xnb", "K_sb": "K_sb", "VT": "VT",
                               "n2b": "n2b", "comb": "comb"}.get(name, "nope"),
                        None)
            if t is not None:
                h = self.nc.dram_tensor("dump_" + name, list(t.ap().shape),
                                        t.ap().dtype, kind="ExternalOutput")
                dumps.append((h, t))

        @blk.sync
        def _(sy):
            self.wait_all(sy)
            for h, t in dumps:
                self.D(ST, sy.dma_start(out=h.ap(), in_=t[:]))
            self.D(ST, sy.dma_start(out=self.out_h.ap()[0:128, 0:128],
                                    in_=self.eye[:]))
            sy.wait_ge(ST.h, ST.n)


def build(dev_stop=None):
    b = Builder()
    with b.ctx:
        def done():
            b.finish_dev()
            for name in ("stP4b", "stP4", "stP3", "stP2", "stBV"):
                stx = getattr(b, name, None)
                if stx is not None:
                    stx.close()
            for name in ("stXNB", "stW0", "stW", "stKV"):
                getattr(b, name).close()
            return b

        b.declare()
        b.consts()
        b.p0_preln()
        if dev_stop == "p0":
            return done()
        b.p1_qkv()
        if dev_stop == "p1":
            return done()
        b.stBV.close()
        b.p2_front()
        if dev_stop == "p2":
            return done()
        b.stXNB.close()
        b.stW0.close()
        b.stW.close()
        b.p3_attn()
        b.stP3.close()
        b.p3_oproj()
        if dev_stop == "p3":
            return done()
        b.stKV.close()
        b.p4_moe()
        if dev_stop == "p4":
            return done()
        b.stP4b.close()
        b.p5_final()
        b.stP4.close()
        b.stP2.close()
        return b


def make_dex(dist_emb):
    M = 1024
    NDE = 2 * M - 1
    u = np.arange(UW)
    g = np.where(u <= M - 1, NDE - 1,
                 np.where(u <= 3 * M - 2, (3 * M - 2) - u, (5 * M - 3) - u))
    g = np.clip(g, 0, NDE - 1)
    g2 = np.where(u <= M - 1, u + (M - 1),
                  np.where(u <= 3 * M - 2, u - M,
                           np.full_like(u, NDE - 1)))
    g2 = np.clip(g2, 0, NDE - 1)
    dexq = np.ascontiguousarray(dist_emb[g].T)
    dexk = np.ascontiguousarray(dist_emb[g2].T)
    return dexq, dexk


def kernel(**inputs):
    x = np.ascontiguousarray(np.asarray(inputs["L0_attn_hidden_states"],
                                        dtype=np.float32))
    dexq, dexk = make_dex(np.asarray(inputs["L0_attn_dist_emb"],
                                     dtype=np.float32))
    shared = {
        "Wq": inputs["L0_attn_Wq"], "Wk": inputs["L0_attn_Wk"],
        "Wv": inputs["L0_attn_Wv"], "Wo": inputs["L0_attn_Wo"],
        "bq": inputs["L0_attn_bq"], "bk": inputs["L0_attn_bk"],
        "bv": inputs["L0_attn_bv"], "bo": inputs["L0_attn_bo"],
        "lng": inputs["L0_attn_ln_gamma"], "lnb": inputs["L0_attn_ln_beta"],
        "g1": inputs["L0_ffn_g1"], "be1": inputs["L0_ffn_be1"],
        "g2": inputs["L0_ffn_g2"], "be2": inputs["L0_ffn_be2"],
        "mixw": inputs["L0_ffn_mix_w"], "mixb": inputs["L0_ffn_mix_b"],
        "rw": inputs["L0_ffn_router_w"],
        "w1": inputs["L0_ffn_w1"], "b1": inputs["L0_ffn_b1"],
        "w2": inputs["L0_ffn_w2"], "b2": inputs["L0_ffn_b2"],
    }
    shared = {k: np.ascontiguousarray(np.asarray(v, dtype=np.float32))
              for k, v in shared.items()}
    shared["dexq"] = dexq
    shared["dexk"] = dexk
    b = build(DEV_STOP)
    ncores = x.shape[0]
    in_maps = []
    for c in range(ncores):
        m = dict(shared)
        m["x"] = np.ascontiguousarray(x[c])
        in_maps.append(m)
    res = run_bass_kernel_spmd(b.nc, in_maps, core_ids=list(range(ncores)))
    out = np.stack([r["out"] for r in res.results], axis=0)
    if DEV_TAPS:
        kernel.taps = [dict(r) for r in res.results]
    return out
